# version 03
# baseline (speedup 1.0000x reference)
import math
import numpy as np
import concourse.bass as bass
import concourse.mybir as mybir
from concourse.bass_utils import run_bass_kernel_spmd
from concourse.ap import AP

F32 = mybir.dt.float32
BF16 = mybir.dt.bfloat16
AF = mybir.ActivationFunctionType
ALU = mybir.AluOpType
AX = mybir.AxisListType

D = 1024
NCH = D // 128
MAGIC = 12582912.0


class Sem:
    def __init__(self, nc, name):
        self.h = nc.alloc_semaphore(name)
        self.n = 0

    def inc(self, instr, k=1):
        instr.then_inc(self.h, k)
        self.n += k
        return instr

    def tok(self):
        return (self, self.n)

    def wait(self, eng):
        eng.wait_ge(self.h, self.n)


def wait_tok(eng, tok):
    sem, n = tok
    eng.wait_ge(sem.h, n)


class Bank:
    def __init__(self, prog, i):
        self.ps = prog.nc.alloc_psum_tensor(f"bank{i}", [128, 512], F32)
        self.free = Sem(prog.nc, f"bkf{i}")
        self.mm = Sem(prog.nc, f"bkm{i}")


class Prog:
    def __init__(self, nc, T):
        assert T % 512 == 0
        self.nc = nc
        self.T = T
        self.NT = T // 512
        self.NTT = T // 128
        self._blk_cm = None
        self.blk = None
        self._phase_i = 0
        self.banks = [Bank(self, i) for i in range(8)]
        self._bank_rr = 0
        self._reserved = set()
        self.sems = {}
        self.SEM_POOL_N = 60
        self._sem_pool = []
        self._sem_next = 0
        self._cpool = nc.alloc_sbuf_tensor("cpool", [128, 32], F32)
        self._cpool_vals = {}
        self._deferred_consts = []
        self.ones_bf = nc.alloc_sbuf_tensor("ones_bf", [128, 128], BF16)
        self.ident = nc.alloc_sbuf_tensor("ident", [128, 128], BF16)
        self.ident_f32 = nc.alloc_sbuf_tensor("ident_f32", [128, 128], F32)
        self.ones_f1 = nc.alloc_sbuf_tensor("ones_f1", [1, 128], F32)
        self.ybuf = nc.alloc_sbuf_tensor("ybuf", [128, NCH, 512], F32)
        self.mu = nc.alloc_sbuf_tensor("mu_t", [128, 512], F32)
        self.msq = nc.alloc_sbuf_tensor("msq_t", [128, 512], F32)
        self.mu2 = nc.alloc_sbuf_tensor("mu2_t", [128, 512], F32)
        self.tscr = nc.alloc_sbuf_tensor("tscr", [128, 512], F32)
        self.xn = nc.alloc_sbuf_tensor("xn", [128, NCH, T], BF16)
        self.scal = nc.alloc_sbuf_tensor("scal", [128, 16], F32)
        self.cst8 = nc.alloc_sbuf_tensor("cst8", [8, 128], F32)
        self.rowstage = nc.alloc_sbuf_tensor("rowstage", [1, 512], F32)
        self.ARENA_BYTES = 110 * 1024
        self.LN_RESERVE = 32 * 1024
        self.arena = nc.alloc_sbuf_tensor("arena", [128, self.ARENA_BYTES], mybir.dt.uint8)
        self._arena_off = self.LN_RESERVE
        self.xbuf = self.arena[:, 0:16384].bitcast(F32).rearrange(
            "p (a b) -> p a b", a=NCH, b=512)
        self.xb16 = self.arena[:, 16384:24576].bitcast(BF16).rearrange(
            "p (a b) -> p a b", a=NCH, b=512)
        self.sqb = self.arena[:, 24576:32768].bitcast(BF16).rearrange(
            "p (a b) -> p a b", a=NCH, b=512)

    def arena_reset(self, full=False):
        self._arena_off = 0 if full else self.LN_RESERVE

    def arena_mark(self):
        return self._arena_off

    def arena_reset_to(self, mark):
        self._arena_off = mark

    def aalloc(self, shape, dtype):
        assert shape[0] == 128
        esz = mybir.dt.size(dtype)
        n = 1
        for s in shape[1:]:
            n *= s
        nbytes = n * esz
        off = (self._arena_off + 511) & ~511
        assert off + nbytes <= self.ARENA_BYTES, (off, nbytes, self.ARENA_BYTES)
        self._arena_off = off + nbytes
        ap = self.arena[:, off:off + nbytes].bitcast(dtype)
        if len(shape) > 2:
            kw = {}
            names = []
            for i, s in enumerate(shape[1:]):
                names.append(f"a{i}")
                kw[f"a{i}"] = s
            ap = ap.rearrange(f"p ({' '.join(names)}) -> p {' '.join(names)}", **kw)
        return ap

    def sem(self, name):
        if name not in self.sems:
            if len(self._sem_pool) < self.SEM_POOL_N:
                s = Sem(self.nc, f"pool{len(self._sem_pool)}")
                self._sem_pool.append(s)
            s = self._sem_pool[self._sem_hash(name)]
            self.sems[name] = s
        return self.sems[name]

    def _sem_hash(self, name):
        i = self._sem_next
        self._sem_next = (self._sem_next + 1) % min(self.SEM_POOL_N, max(1, len(self._sem_pool)))
        return i % len(self._sem_pool)

    def constap(self, val):
        val = float(val)
        if val not in self._cpool_vals:
            i = len(self._cpool_vals)
            assert i < 32
            self._cpool_vals[val] = i
            self._deferred_consts.append((i, val))
        i = self._cpool_vals[val]
        return self._cpool[:, i:i + 1]

    def reg_const(self, val):
        ap = self.constap(val)
        try:
            self.nc.const_aps.register(F32, float(val), ap)
        except Exception:
            pass
        return ap

    def phase(self):
        if self._blk_cm is not None:
            self._blk_cm.__exit__(None, None, None)
        self._phase_i += 1
        self._blk_cm = self.nc.Block()
        self.blk = self._blk_cm.__enter__()
        return self.blk

    def finish(self):
        if self._blk_cm is not None:
            self._blk_cm.__exit__(None, None, None)
            self._blk_cm = None

    def bank(self):
        for _ in range(len(self.banks)):
            b = self.banks[self._bank_rr]
            self._bank_rr = (self._bank_rr + 1) % len(self.banks)
            if id(b) not in self._reserved:
                return b
        raise RuntimeError("all banks reserved")

    def reserve_banks(self, n):
        out = []
        while len(out) < n:
            b = self.bank()
            self._reserved.add(id(b))
            out.append(b)
        return out

    def release_banks(self, bs):
        for b in bs:
            self._reserved.discard(id(b))

    def mm_acc(self, bank, cols, pairs, waits=(), out_off=0):
        def f(t):
            for tk in waits:
                wait_tok(t, tk)
            bank.free.wait(t)
            ins = None
            n = len(pairs)
            for i, (l, r) in enumerate(pairs):
                ins = t.matmul(bank.ps[:, out_off:out_off + cols], l, r,
                               start=(i == 0), stop=(i == n - 1))
            bank.mm.inc(ins)
        self.blk.tensor(f)

    def evac(self, bank, engine, fn):
        out = {}

        def f(eng):
            bank.mm.wait(eng)
            last = fn(eng)
            bank.free.inc(last)
            out["tok"] = bank.free.tok()
        getattr(self.blk, engine)(f)
        return out["tok"]

    def init_phase(self):
        blk = self.phase()
        consts = list(self._deferred_consts)

        def fg(gp):
            gp.memset(self.ones_bf[:], 1.0)
            gp.memset(self.ones_f1[:], 1.0)
            gp.memset(self.ident[:], 0.0)
            for i, v in consts:
                gp.memset(self._cpool[:, i:i + 1], v)
            gp.affine_select(self.ident[:], self.ones_bf[:], [[1, 128]],
                             ALU.is_equal, 0.0, base=0, channel_multiplier=-1)
            gp.tensor_copy(self.ident_f32[:], self.ident[:])
        blk.gpsimd(fg)
        self.phase()

    def bcast_scalar(self, src11, dst_col, waits=()):
        b = self.bank()

        def f(t):
            for tk in waits:
                wait_tok(t, tk)
            b.free.wait(t)
            ins = t.matmul(b.ps[:, 0:1], self.ones_f1[:], src11, start=True, stop=True)
            b.mm.inc(ins)
        self.blk.tensor(f)
        return self.evac(b, "vector", lambda v: v.tensor_copy(dst_col, b.ps[:, 0:1]))


def dram_fm(dram, T, t0, tlen, nch=NCH, dc0=0):
    return AP(dram, dc0 * 128 * T + t0, [[T, 128], [128 * T, nch], [1, tlen]])


def dram_row(dram, T, dc, t0, tlen):
    return AP(dram, dc * 128 * T + t0, [[T, 128], [1, tlen]])



def emit_ln_in(prog, src_dram, eps=1e-5):
    nc = prog.nc
    T = prog.T
    for tc in range(prog.NT):
        t0 = tc * 512
        blk = prog.phase()
        ld = prog.sem("ld")
        cast = prog.sem("cast")
        sq = prog.sem("sq")

        def fdma(sy, t0=t0):
            ld.inc(sy.dma_start(prog.xbuf[:], dram_fm(src_dram, T, t0, 512)), 16)
        blk.sync(fdma)

        def fcast(v):
            ld.wait(v)
            ins = None
            for dc in range(NCH):
                ins = v.tensor_copy(prog.xb16[:, dc], prog.xbuf[:, dc])
            cast.inc(ins)
        blk.vector(fcast)

        def fsq(sc):
            cast.wait(sc)
            ins = None
            for dc in range(NCH):
                ins = sc.activation(prog.sqb[:, dc], prog.xb16[:, dc], AF.Square)
            sq.inc(ins)
        blk.scalar(fsq)

        b1 = prog.bank()
        b2 = prog.bank()
        prog.mm_acc(b1, 512, [(prog.ones_bf[:], prog.xb16[:, dc]) for dc in range(NCH)],
                    waits=[cast.tok()])
        prog.mm_acc(b2, 512, [(prog.ones_bf[:], prog.sqb[:, dc]) for dc in range(NCH)],
                    waits=[sq.tok()])
        tok1 = prog.evac(b1, "scalar", lambda sc, b=b1: sc.activation(
            prog.mu[:], b.ps[:], AF.Copy, scale=1.0 / D))
        tok2 = prog.evac(b2, "scalar", lambda sc, b=b2: sc.activation(
            prog.msq[:], b.ps[:], AF.Copy, scale=1.0 / D))

        ve = prog.sem("lnve")

        def fvar(v):
            wait_tok(v, tok1)
            wait_tok(v, tok2)
            v.tensor_mul(prog.mu2[:], prog.mu[:], prog.mu[:])
            ve.inc(v.tensor_sub(prog.mu2[:], prog.msq[:], prog.mu2[:]))
        blk.vector(fvar)

        def fsqr(sc):
            ve.wait(sc)
            ve.inc(sc.activation(prog.mu2[:], prog.mu2[:], AF.Sqrt, bias=float(eps)))
        blk.scalar(fsqr)

        def frec(v):
            ve.wait(v)
            ve.inc(v.reciprocal(prog.mu2[:], prog.mu2[:]))
        blk.vector(fvec := frec)

        def fapply(v, t0=t0):
            ve.wait(v)
            ins = None
            for dc in range(NCH):
                v.tensor_sub(prog.tscr[:], prog.xbuf[:, dc], prog.mu[:])
                ins = v.tensor_mul(prog.xn[:, dc, t0:t0 + 512], prog.tscr[:], prog.mu2[:])
            ve.inc(ins)
        blk.vector(fapply)
    prog.phase()


def emit_ln_out_residual(prog, y_dram, x_dram, eps=1e-5):
    nc = prog.nc
    T = prog.T
    for tc in range(prog.NT):
        t0 = tc * 512
        blk = prog.phase()
        ld = prog.sem("ld")
        cast = prog.sem("cast")
        sq = prog.sem("sq")
        ve = prog.sem("lnve")
        st = prog.sem("st")

        def fdma(sy, t0=t0):
            ld.inc(sy.dma_start(prog.ybuf[:], dram_fm(y_dram, T, t0, 512)), 16)
            ld.inc(sy.dma_start(prog.xbuf[:], dram_fm(x_dram, T, t0, 512)), 16)
        blk.sync(fdma)

        def fcast(v):
            ld.wait(v)
            ins = None
            for dc in range(NCH):
                ins = v.tensor_copy(prog.xb16[:, dc], prog.ybuf[:, dc])
            cast.inc(ins)
        blk.vector(fcast)

        def fsq(sc):
            cast.wait(sc)
            ins = None
            for dc in range(NCH):
                ins = sc.activation(prog.sqb[:, dc], prog.xb16[:, dc], AF.Square)
            sq.inc(ins)
        blk.scalar(fsq)

        b1 = prog.bank()
        b2 = prog.bank()
        prog.mm_acc(b1, 512, [(prog.ones_bf[:], prog.xb16[:, dc]) for dc in range(NCH)],
                    waits=[cast.tok()])
        prog.mm_acc(b2, 512, [(prog.ones_bf[:], prog.sqb[:, dc]) for dc in range(NCH)],
                    waits=[sq.tok()])
        tok1 = prog.evac(b1, "scalar", lambda sc, b=b1: sc.activation(
            prog.mu[:], b.ps[:], AF.Copy, scale=1.0 / D))
        tok2 = prog.evac(b2, "scalar", lambda sc, b=b2: sc.activation(
            prog.msq[:], b.ps[:], AF.Copy, scale=1.0 / D))

        def fvar(v):
            wait_tok(v, tok1)
            wait_tok(v, tok2)
            v.tensor_mul(prog.mu2[:], prog.mu[:], prog.mu[:])
            ve.inc(v.tensor_sub(prog.mu2[:], prog.msq[:], prog.mu2[:]))
        blk.vector(fvar)

        def fsqr(sc):
            ve.wait(sc)
            ve.inc(sc.activation(prog.mu2[:], prog.mu2[:], AF.Sqrt, bias=float(eps)))
        blk.scalar(fsqr)

        def frec(v):
            ve.wait(v)
            ve.inc(v.reciprocal(prog.mu2[:], prog.mu2[:]))
        blk.vector(frec)

        def fapply(v, t0=t0):
            ve.wait(v)
            ins = None
            for dc in range(NCH):
                v.tensor_sub(prog.tscr[:], prog.ybuf[:, dc], prog.mu[:])
                v.tensor_mul(prog.tscr[:], prog.tscr[:], prog.mu2[:])
                ins = v.tensor_add(prog.xbuf[:, dc], prog.xbuf[:, dc], prog.tscr[:])
            ve.inc(ins)
        blk.vector(fapply)

        def fst(sy, t0=t0):
            ve.wait(sy)
            st.inc(sy.dma_start(dram_fm(x_dram, T, t0, 512), prog.xbuf[:]), 16)
        blk.sync(fst)
    blk = prog.phase()
    st = prog.sem("st")

    def fw(sy):
        st.wait(sy)
    blk.sync(fw)
    prog.phase()


def emit_dump_xn(prog, yscr):
    T = prog.T
    blk = prog.phase()
    dbg = prog.sem("dbg")
    cv = prog.sem("dbgcv")
    for dc in range(NCH):
        for tc in range(prog.NT):
            def fcv(v, dc=dc, tc=tc):
                dbg.wait(v)
                cv.inc(v.tensor_copy(prog.tscr[:], prog.xn[:, dc, tc * 512:(tc + 1) * 512]))
            blk.vector(fcv)

            def fdbg(sy, dc=dc, tc=tc):
                cv.wait(sy)
                dbg.inc(sy.dma_start(dram_row(yscr, T, dc, tc * 512, 512), prog.tscr[:]), 16)
            blk.sync(fdbg)
    prog.phase()



def declare_wfm(nc, name, d_in, d_out):
    return nc.dram_tensor(name, (d_out // 128, 128, d_in // 128, 128),
                          F32, kind="ExternalInput")


class WeightFM:

    def __init__(self, prog, dram=None, dtype=BF16, n_uc=None, n_dc=None):
        if dram is not None:
            self.n_uc, _, self.n_dc, _ = dram.shape
        else:
            self.n_uc, self.n_dc = n_uc, n_dc
        self.dram = dram
        self.dt = dtype
        self.sb = prog.aalloc([128, self.n_uc * self.n_dc * 128], dtype)
        if not hasattr(prog, "_ws_rr"):
            prog._ws_rr = 0
        self.sem = prog.sem(f"ws{prog._ws_rr % 8}")
        prog._ws_rr += 1

    def load(self, prog):
        k = self.n_dc * 128

        def f(eng):
            for uc in range(self.n_uc):
                self.sem.inc(eng.dma_start(
                    self.sb[:, uc * k:(uc + 1) * k], self.dram[uc]), 16)
        if self.dt == F32:
            prog.blk.sync(f)
        else:
            prog.blk.gpsimd(f)
        return self.sem.tok()

    def tile(self, uc, dc):
        k = self.n_dc * 128
        return self.sb[:, uc * k + dc * 128: uc * k + (dc + 1) * 128]


def host_wfm(w):
    w = np.asarray(w)
    d_in, d_out = w.shape
    return np.ascontiguousarray(
        w.reshape(d_in // 128, 128, d_out // 128, 128).transpose(2, 1, 0, 3)
    ).astype(np.float32)


def proj_fm(prog, w, xin_of, epilogue, uc_list=None, tc_list=None, waits=()):
    uc_list = range(w.n_uc) if uc_list is None else uc_list
    tc_list = range(prog.NT) if tc_list is None else tc_list
    toks = {}
    for uc in uc_list:
        for tc in tc_list:
            b = prog.bank()
            prog.mm_acc(b, 512,
                        [(w.tile(uc, dc), xin_of(dc, tc)) for dc in range(w.n_dc)],
                        waits=list(waits) + [w.sem.tok()])
            toks[(uc, tc)] = epilogue(uc, tc, b)
    return toks



def emit_l0_ffn(prog, W, debug_stop=None):
    nc = prog.nc
    T = prog.T
    gb_col = W["l0ffn_gb"]
    yscr = W["yscr"]

    prog.arena_reset()
    wf = WeightFM(prog, dram=W["L0_ffn_weight_dram"], dtype=F32)
    w = WeightFM(prog, dtype=BF16, n_uc=8, n_dc=8)
    blk = prog.phase()
    wf.load(prog)
    if debug_stop == "ffn_gb":
        blk = prog.phase()
        dbg = prog.sem("dbg")
        cv = prog.sem("dbgcv")

        def fcv(v):
            cv.inc(v.tensor_copy(prog.tscr[:, 0:16], gb_col[:, 0:16]))
        blk.vector(fcv)

        def fdbg(sy):
            cv.wait(sy)
            dbg.inc(sy.dma_start(AP(yscr, 0, [[T, 128], [1, 16]]), prog.tscr[:, 0:16]), 16)
            dbg.wait(sy)
        blk.sync(fdbg)
        return

    for tc in range(prog.NT):
        t0 = tc * 512
        blk = prog.phase()
        sq = prog.sem("sq")
        ve = prog.sem("lnve")

        def fsq(sc, t0=t0):
            ins = None
            for dc in range(NCH):
                ins = sc.activation(prog.sqb[:, dc], prog.xn[:, dc, t0:t0 + 512], AF.Square)
            sq.inc(ins)
        blk.scalar(fsq)
        b1 = prog.bank()
        b2 = prog.bank()
        prog.mm_acc(b1, 512, [(prog.ones_bf[:], prog.xn[:, dc, t0:t0 + 512])
                              for dc in range(NCH)])
        prog.mm_acc(b2, 512, [(prog.ones_bf[:], prog.sqb[:, dc]) for dc in range(NCH)],
                    waits=[sq.tok()])
        tok1 = prog.evac(b1, "scalar", lambda sc, b=b1: sc.activation(
            prog.mu[:], b.ps[:], AF.Copy, scale=1.0 / D))
        tok2 = prog.evac(b2, "scalar", lambda sc, b=b2: sc.activation(
            prog.msq[:], b.ps[:], AF.Copy, scale=1.0 / D))

        def fvar(v):
            wait_tok(v, tok1)
            wait_tok(v, tok2)
            v.tensor_mul(prog.mu2[:], prog.mu[:], prog.mu[:])
            ve.inc(v.tensor_sub(prog.mu2[:], prog.msq[:], prog.mu2[:]))
        blk.vector(fvar)

        def fsqr(sc):
            ve.wait(sc)
            ve.inc(sc.activation(prog.mu2[:], prog.mu2[:], AF.Sqrt, bias=1e-3))
        blk.scalar(fsqr)

        def frec(v):
            ve.wait(v)
            ve.inc(v.reciprocal(prog.mu2[:], prog.mu2[:]))
        blk.vector(frec)

        def fap(v, t0=t0):
            ve.wait(v)
            ins = None
            for dc in range(NCH):
                v.tensor_sub(prog.tscr[:], prog.xn[:, dc, t0:t0 + 512], prog.mu[:])
                v.tensor_mul(prog.tscr[:], prog.tscr[:], prog.mu2[:])
                ins = v.tensor_scalar(prog.xn[:, dc, t0:t0 + 512], prog.tscr[:],
                                      gb_col[:, 2 * dc:2 * dc + 1],
                                      gb_col[:, 2 * dc + 1:2 * dc + 2],
                                      ALU.mult, ALU.add)
            ve.inc(ins)
        blk.vector(fap)
    prog.phase()
    if debug_stop == "ffn_ln2":
        emit_dump_xn(prog, yscr)
        return

    blk = prog.phase()
    red = prog.sem("red")
    amax_col = prog.scal[:, 0:1]
    wabs_col = prog.scal[:, 1:2]

    def fred(v):
        wf.sem.wait(v)
        ins = v.tensor_reduce(amax_col, prog.xn[:, 0, :], AX.X, ALU.max,
                              apply_absolute_value=True)
        red.inc(ins)
        v.wait_ge(red.h, red.n)
        for dc in range(1, NCH):
            ins = v.tensor_reduce(prog.scal[:, 2:3], prog.xn[:, dc, :], AX.X, ALU.max,
                                  apply_absolute_value=True)
            red.inc(ins)
            v.wait_ge(red.h, red.n)
            ins = v.tensor_max(amax_col, amax_col, prog.scal[:, 2:3])
            red.inc(ins)
            v.wait_ge(red.h, red.n)
        nn = wf.n_uc * wf.n_dc * 128
        half = nn // 2
        ins = v.tensor_reduce(wabs_col, wf.sb[:, 0:half], AX.X, ALU.add,
                              apply_absolute_value=True)
        red.inc(ins)
        v.wait_ge(red.h, red.n)
        ins = v.tensor_reduce(prog.scal[:, 2:3], wf.sb[:, half:nn], AX.X, ALU.add,
                              apply_absolute_value=True)
        red.inc(ins)
        v.wait_ge(red.h, red.n)
        ins = v.tensor_add(wabs_col, wabs_col, prog.scal[:, 2:3])
        red.inc(ins)
    blk.vector(fred)

    pscr = W["pscr"]
    blk = prog.phase()
    cs = prog.sem("cc")
    rowbuf = W["rowbuf"]

    def fd1(sy):
        red.wait(sy)
        cs.inc(sy.dma_start(AP(pscr, 0, [[1, 128], [1, 1]]), amax_col), 16)
        cs.inc(sy.dma_start(AP(pscr, 128, [[1, 128], [1, 1]]), wabs_col), 16)
    blk.sync(fd1)

    def fd2(sy):
        cs.wait(sy)
        cs.inc(sy.dma_start(rowbuf[:], AP(pscr, 0, [[256, 1], [1, 256]])), 16)
    blk.sync(fd2)

    blk = prog.phase()
    ccin = W["ccin"]
    cc_in_d = W["cc_in_d"]
    cc_out_d = W["cc_out_d"]

    def fr2(v):
        cs.wait(v)
        ins = v.tensor_reduce(ccin[0:1, 0:1], rowbuf[0:1, 0:128], AX.X, ALU.max)
        red.inc(ins)
        v.wait_ge(red.h, red.n)
        ins = v.tensor_reduce(ccin[0:1, 1:2], rowbuf[0:1, 128:256], AX.X, ALU.add)
        red.inc(ins)
        v.wait_ge(red.h, red.n)
        ins = v.memset(ccin[0:1, 2:16], 0.0)
        red.inc(ins)
    blk.vector(fr2)

    def fd3(sy):
        red.wait(sy)
        cs.inc(sy.dma_start(cc_in_d[:], ccin[:]), 16)
    blk.sync(fd3)

    def fcc(gp):
        cs.wait(gp)
        cs.inc(gp.collective_compute("AllReduce", ALU.max,
                                     [[0, 1, 2, 3, 4, 5, 6, 7]],
                                     [cc_in_d.ap().opt()], [cc_out_d.ap().opt()]), 1)
    blk.gpsimd(fcc)

    def fd4(sy):
        cs.wait(sy)
        cs.inc(sy.dma_start(ccin[:], cc_out_d[:]), 16)
    blk.sync(fd4)

    blk = prog.phase()

    def fsy(sy):
        cs.wait(sy)
    blk.sync(fsy)
    g_tok = prog.bcast_scalar(ccin[0:1, 0:1], prog.scal[:, 3:4], waits=[cs.tok()])
    b_tok = prog.bcast_scalar(ccin[0:1, 1:2], prog.scal[:, 4:5], waits=[cs.tok()])
    blk = prog.phase()
    sca = prog.sem("sca")

    def fsc(v):
        def step(ins):
            sca.inc(ins)
            v.wait_ge(sca.h, sca.n)
        wait_tok(v, g_tok)
        wait_tok(v, b_tok)
        step(v.tensor_scalar(prog.scal[:, 4:5], prog.scal[:, 4:5], 1.0 / (D * D), 1e-5,
                             ALU.mult, ALU.add))
        step(v.tensor_scalar(prog.scal[:, 6:7], prog.scal[:, 4:5], -1e-5, None, ALU.add))
        step(v.tensor_mul(prog.scal[:, 6:7], prog.scal[:, 6:7], prog.scal[:, 3:4]))
        step(v.tensor_scalar_mul(prog.scal[:, 6:7], prog.scal[:, 6:7], 1.0 / 128.0))
        step(v.reciprocal(prog.scal[:, 4:5], prog.scal[:, 4:5]))
        step(v.reciprocal(prog.scal[:, 5:6], prog.scal[:, 3:4]))
        step(v.tensor_scalar_mul(prog.scal[:, 5:6], prog.scal[:, 5:6], 128.0))
    blk.vector(fsc)

    blk = prog.phase()
    wqscr = prog.aalloc([128, 2048], F32)

    def fwq(v):
        sca.wait(v)
        nn = wf.n_uc * wf.n_dc * 128
        step = 2048
        ins = None
        for i in range(0, nn, step):
            v.tensor_scalar(wqscr[:, 0:step], wf.sb[:, i:i + step],
                            prog.scal[:, 4:5], MAGIC, ALU.mult, ALU.add)
            v.tensor_scalar(wqscr[:, 0:step], wqscr[:, 0:step], -MAGIC, 1.0,
                            ALU.add, ALU.min)
            ins = v.tensor_scalar(w.sb[:, i:i + step], wqscr[:, 0:step], -1.0, None,
                                  ALU.max)
        w.sem.inc(ins)
    blk.vector(fwq)

    blk = prog.phase()
    xq = prog.sem("xq")
    LIM = 128.0 - 1e-5

    def fxq(v):
        sca.wait(v)
        ins = None
        for dc in range(NCH):
            for tc in range(prog.NT):
                sl = prog.xn[:, dc, tc * 512:(tc + 1) * 512]
                v.tensor_scalar(prog.tscr[:], sl, prog.scal[:, 5:6], LIM,
                                ALU.mult, ALU.min)
                ins = v.tensor_scalar(sl, prog.tscr[:], -LIM, None, ALU.max)
        xq.inc(ins)
    blk.vector(fxq)
    if debug_stop == "ffn_xq":
        prog.phase()
        emit_dump_xn(prog, yscr)
        return
    if debug_stop == "ffn_scal":
        blk = prog.phase()
        dbg = prog.sem("dbg")

        def fdbg(sy):
            dbg.inc(sy.dma_start(AP(yscr, 0, [[16, 128], [1, 16]]),
                                 prog.scal[:, 0:16]), 16)
            dbg.wait(sy)
        blk.sync(fdbg)
        return

    blk = prog.phase()
    st = prog.sem("st")
    ev = prog.sem("ev")

    for tc in range(prog.NT):
        t0 = tc * 512
        st_before = st.n

        def epi(uc, tc_, b, st_before=st_before):
            def f(sc, uc=uc, b=b):
                sc.wait_ge(st.h, st_before)
                return sc.activation(prog.ybuf[:, uc], b.ps[:], AF.Copy,
                                     scale=prog.scal[:, 6:7])
            return prog.evac(b, "scalar", f)
        toks = proj_fm(prog, w, lambda dc, tcc: prog.xn[:, dc, t0:t0 + 512],
                       epi, tc_list=[tc], waits=[xq.tok()])

        def fst(sy, t0=t0, toks=tuple(toks.values())):
            for tk in toks:
                wait_tok(sy, tk)
            st.inc(sy.dma_start(dram_fm(yscr, T, t0, 512), prog.ybuf[:]), 16)
        blk.sync(fst)

    blk = prog.phase()

    def fw2(sy):
        st.wait(sy)
    blk.sync(fw2)
    prog.phase()



def emit_l1_ffn(prog, W, debug_stop=None):
    nc = prog.nc
    T = prog.T
    yscr = W["yscr"]
    h2scr = W["h2scr"]
    h3scr = W["h3scr"]
    h3part = W["h3part"]
    guscr = W["guscr"]
    gushscr = W["gushscr"]
    E = 8

    ld = prog.sem("l1ld")
    st = prog.sem("l1st")
    ev = prog.sem("l1ev")

    def drain(sems):
        blk = prog.phase()

        def f(sy):
            for s in sems:
                s.wait(sy)
        blk.sync(f)

    prog.arena_reset(full=True)
    w_lin0 = WeightFM(prog, W["L1_ffn_lin0_w"])
    w_s1a = WeightFM(prog, W["L1_ffn_swi1_w1"])
    w_s1b = WeightFM(prog, W["L1_ffn_swi1_w2"])
    h1t = prog.aalloc([128, NCH, 512], BF16)
    h2t = prog.aalloc([128, NCH, 512], BF16)
    gbuf = prog.aalloc([128, NCH, 512], F32)
    bcols = prog.aalloc([128, 64], F32)
    gateT = prog.aalloc([128, 64], BF16)
    cmb = prog.aalloc([128, 8], F32)
    escr = prog.aalloc([128, 8], F32)
    srt8 = prog.aalloc([128, 8], F32)
    scol = prog.aalloc([128, 4], F32)
    csdram = W["csdram"]
    csem = prog.sem("l1csem")

    blk = prog.phase()
    w_lin0.load(prog)
    w_s1a.load(prog)
    w_s1b.load(prog)

    def fld(gp):
        ld.inc(gp.dma_start(bcols[:, 0:48], W["L1_bcols"][:, 0:48]), 16)
        ld.inc(gp.dma_start(gateT[:], W["L1_gateT"][:]), 16)
    blk.gpsimd(fld)
    B0, B1A, B1B, B2A, B2B, BL1 = 0, 8, 16, 24, 32, 40

    for tc in range(prog.NT):
        t0 = tc * 512
        blk = prog.phase()
        toks1 = []
        for uc in range(NCH):
            b = prog.bank()
            prog.mm_acc(b, 512, [(w_lin0.tile(uc, dc), prog.xn[:, dc, t0:t0 + 512])
                                 for dc in range(NCH)],
                        waits=[w_lin0.sem.tok(), ld.tok()])
            toks1.append(prog.evac(b, "scalar", lambda sc, b=b, uc=uc: sc.activation(
                h1t[:, uc], b.ps[:], AF.Identity, bias=bcols[:, B0 + uc:B0 + uc + 1])))
        toks2 = []
        for uc in range(NCH):
            bg = prog.bank()
            prog.mm_acc(bg, 512, [(w_s1a.tile(uc, dc), h1t[:, dc]) for dc in range(NCH)],
                        waits=[w_s1a.sem.tok()] + toks1)
            tg = prog.evac(bg, "scalar", lambda sc, b=bg, uc=uc: sc.activation(
                gbuf[:, uc], b.ps[:], AF.Silu, bias=bcols[:, B1A + uc:B1A + uc + 1]))
            bu = prog.bank()
            prog.mm_acc(bu, 512, [(w_s1b.tile(uc, dc), h1t[:, dc]) for dc in range(NCH)],
                        waits=[w_s1b.sem.tok()] + toks1)

            def fu(v, b=bu, uc=uc, tg=tg):
                wait_tok(v, tg)
                return v.scalar_tensor_tensor(h2t[:, uc], b.ps[:],
                                              bcols[:, B1B + uc:B1B + uc + 1],
                                              gbuf[:, uc], ALU.add, ALU.mult)
            toks2.append(prog.evac(bu, "vector", fu))
        def fst(sy, t0=t0, toks=tuple(toks2)):
            for tk in toks:
                wait_tok(sy, tk)
            st.inc(sy.dma_start(dram_fm(h2scr, T, t0, 512), h2t[:]), 16)
        blk.sync(fst)

        for tt in range(4):
            p0 = t0 + tt * 128
            b = prog.bank()
            prog.mm_acc(b, 8, [(h2t[:, dc, tt * 128:(tt + 1) * 128],
                                gateT[:, dc * 8:(dc + 1) * 8]) for dc in range(NCH)],
                        waits=list(toks2))
            ch = prog.sem("l1gate")

            def fsm(v, b=b):
                b.mm.wait(v)
                ins = v.reduce_max(scol[:, 0:1], b.ps[:, 0:8], AX.X, negate=True)
                ch.inc(ins)
                v.wait_ge(ch.h, ch.n)
                ins = v.tensor_scalar(escr[:], b.ps[:, 0:8], scol[:, 0:1], None, ALU.add)
                ch.inc(ins)
                b.free.inc(v.engine_nop())
            blk.vector(fsm)

            def fexp(sc):
                ch.wait(sc)
                ch.inc(sc.activation(escr[:], escr[:], AF.Exp))
            blk.scalar(fexp)

            def ftop(v, p0=p0):
                ch.wait(v)

                def step(ins):
                    ch.inc(ins)
                    v.wait_ge(ch.h, ch.n)
                step(v.max(out=srt8[:], in_=escr[:]))
                step(v.reduce_sum(scol[:, 1:2], srt8[:, 0:4], AX.X))
                step(v.reciprocal(scol[:, 2:3], scol[:, 1:2]))
                step(v.tensor_scalar(cmb[:], escr[:], srt8[:, 3:4], None, ALU.is_ge))
                step(v.tensor_mul(cmb[:], cmb[:], escr[:]))
                step(v.tensor_scalar(cmb[:], cmb[:], scol[:, 2:3], None, ALU.mult))
            blk.vector(ftop)

            bt = prog.bank()

            def ftr(t, bt=bt):
                ch.wait(t)
                bt.free.wait(t)
                bt.mm.inc(t.transpose(bt.ps[0:8, 0:128], cmb[:], prog.ident_f32[:]))
            blk.tensor(ftr)
            tcs = prog.evac(bt, "vector", lambda v, bt=bt: (
                wait_tok(v, (csem, csem.n)),
                v.tensor_copy(prog.cst8[:], bt.ps[0:8, 0:128]))[-1])

            def fcst(sy, p0=p0, tcs=tcs):
                wait_tok(sy, tcs)
                csem.inc(sy.dma_start(AP(csdram, p0, [[T, 8], [1, 128]]),
                                      prog.cst8[:]), 16)
            blk.sync(fcst)
    drain([st])
    if debug_stop == "l1_h2":
        emit_copy_scr(prog, h2scr, yscr, dt=BF16)
        return

    for half in range(2):
        prog.arena_reset(full=True)
        w1h = WeightFM(prog, dram=None, dtype=BF16, n_uc=8, n_dc=8)
        w3h = WeightFM(prog, dram=None, dtype=BF16, n_uc=8, n_dc=8)
        rhs = prog.aalloc([128, NCH, 512], BF16)
        gut = prog.aalloc([128, NCH, 512], BF16)
        gbuf2 = prog.aalloc([128, NCH, 512], F32)
        blk = prog.phase()

        def fldw(gp, half=half, w1h=w1h, w3h=w3h):
            k = 8 * 128
            for uc in range(8):
                w1h.sem.inc(gp.dma_start(w1h.sb[:, uc * k:(uc + 1) * k],
                                         W["L1_ffn_sh_w1"][half * 8 + uc]), 16)
                w3h.sem.inc(gp.dma_start(w3h.sb[:, uc * k:(uc + 1) * k],
                                         W["L1_ffn_sh_w3"][half * 8 + uc]), 16)
        blk.gpsimd(fldw)

        for tc in range(prog.NT):
            t0 = tc * 512
            blk = prog.phase()

            def fldr(sy, t0=t0):
                ld.inc(sy.dma_start(rhs[:], dram_fm(h2scr, T, t0, 512)), 16)
            blk.sync(fldr)
            toks = []
            for uc in range(8):
                bg = prog.bank()
                prog.mm_acc(bg, 512, [(w1h.tile(uc, dc), rhs[:, dc]) for dc in range(NCH)],
                            waits=[w1h.sem.tok(), ld.tok()])
                tg = prog.evac(bg, "scalar", lambda sc, b=bg, uc=uc: sc.activation(
                    gbuf2[:, uc], b.ps[:], AF.Silu))
                bu = prog.bank()
                prog.mm_acc(bu, 512, [(w3h.tile(uc, dc), rhs[:, dc]) for dc in range(NCH)],
                            waits=[w3h.sem.tok(), ld.tok()])

                def fu(v, b=bu, uc=uc, tg=tg):
                    wait_tok(v, tg)
                    return v.tensor_mul(gut[:, uc], b.ps[:], gbuf2[:, uc])
                toks.append(prog.evac(bu, "vector", fu))

            def fst(sy, t0=t0, half=half, toks=tuple(toks)):
                for tk in toks:
                    wait_tok(sy, tk)
                st.inc(sy.dma_start(
                    dram_fm(gushscr, T, t0, 512, nch=NCH, dc0=half * 8), gut[:]), 16)
            blk.sync(fst)
        drain([st])

    for e in range(E):
        prog.arena_reset(full=True)
        w1e = WeightFM(prog, dram=None, dtype=BF16, n_uc=8, n_dc=8)
        w3e = WeightFM(prog, dram=None, dtype=BF16, n_uc=8, n_dc=8)
        rhs = prog.aalloc([128, NCH, 512], BF16)
        gut = prog.aalloc([128, NCH, 512], BF16)
        gbuf2 = prog.aalloc([128, NCH, 512], F32)
        cbc = prog.aalloc([128, 512], F32)
        blk = prog.phase()

        def fldw(gp, e=e, w1e=w1e, w3e=w3e):
            k = 8 * 128
            for uc in range(8):
                w1e.sem.inc(gp.dma_start(w1e.sb[:, uc * k:(uc + 1) * k],
                                         W["L1_ffn_exp_w1"][e, uc]), 16)
                w3e.sem.inc(gp.dma_start(w3e.sb[:, uc * k:(uc + 1) * k],
                                         W["L1_ffn_exp_w3"][e, uc]), 16)
        blk.gpsimd(fldw)

        for tc in range(prog.NT):
            t0 = tc * 512
            blk = prog.phase()

            def fldr(sy, t0=t0):
                ld.inc(sy.dma_start(rhs[:], dram_fm(h2scr, T, t0, 512)), 16)
            blk.sync(fldr)
            csg = prog.sem("l1csg")

            def fcsg(sy, e=e, t0=t0):
                csg.inc(sy.dma_start(prog.rowstage[:],
                                     AP(csdram, e * T + t0, [[512, 1], [1, 512]])), 16)
            blk.sync(fcsg)
            bb = prog.bank()

            def fbc(t, bb=bb):
                csg.wait(t)
                bb.free.wait(t)
                bb.mm.inc(t.matmul(bb.ps[:], prog.ones_f1[:],
                                   prog.rowstage[0:1, 0:512], start=True, stop=True))
            blk.tensor(fbc)
            tcb = prog.evac(bb, "vector", lambda v, bb=bb: v.tensor_copy(cbc[:], bb.ps[:]))

            toks = []
            for uc in range(8):
                bg = prog.bank()
                prog.mm_acc(bg, 512, [(w1e.tile(uc, dc), rhs[:, dc]) for dc in range(NCH)],
                            waits=[w1e.sem.tok(), ld.tok()])
                tg = prog.evac(bg, "scalar", lambda sc, b=bg, uc=uc: sc.activation(
                    gbuf2[:, uc], b.ps[:], AF.Silu))
                bu = prog.bank()
                prog.mm_acc(bu, 512, [(w3e.tile(uc, dc), rhs[:, dc]) for dc in range(NCH)],
                            waits=[w3e.sem.tok(), ld.tok()])

                def fu(v, b=bu, uc=uc, tg=tg, tcb=tcb):
                    wait_tok(v, tg)
                    wait_tok(v, tcb)
                    v.tensor_mul(gbuf2[:, uc], b.ps[:], gbuf2[:, uc])
                    return v.tensor_mul(gut[:, uc], gbuf2[:, uc], cbc[:])
                toks.append(prog.evac(bu, "vector", fu))

            def fst(sy, t0=t0, e=e, toks=tuple(toks)):
                for tk in toks:
                    wait_tok(sy, tk)
                st.inc(sy.dma_start(
                    AP(guscr, e * D * T + t0, [[T, 128], [128 * T, NCH], [1, 512]]),
                    gut[:]), 16)
            blk.sync(fst)
        drain([st])
    if debug_stop == "l1_gu":
        emit_copy_scr(prog, guscr, yscr, dt=BF16)
        return

    NPASS = 5
    for p in range(NPASS):
        prog.arena_reset(full=True)
        n_dc_p = 16
        wp = WeightFM(prog, dram=None, dtype=BF16, n_uc=8, n_dc=n_dc_p)
        rhs = prog.aalloc([128, n_dc_p, 512], BF16)
        h3t = prog.aalloc([128, NCH, 512], BF16)
        part = prog.aalloc([128, NCH, 512], F32)
        blk = prog.phase()
        src_w = W["L1_shw2"] if p == NPASS - 1 else W[f"L1_w2cat{p}"]

        def fldw(gp, wp=wp, src_w=src_w, n_dc_p=n_dc_p):
            k = n_dc_p * 128
            for uc in range(8):
                wp.sem.inc(gp.dma_start(wp.sb[:, uc * k:(uc + 1) * k], src_w[uc]), 16)
        blk.gpsimd(fldw)

        for tc in range(prog.NT):
            t0 = tc * 512
            blk = prog.phase()

            def fldr(sy, t0=t0, p=p, n_dc_p=n_dc_p):
                if p < NPASS - 1:
                    for i in range(2):
                        e = p * 2 + i
                        ld.inc(sy.dma_start(
                            rhs[:, i * 8:(i + 1) * 8],
                            AP(guscr, e * D * T + t0, [[T, 128], [128 * T, NCH], [1, 512]])), 16)
                else:
                    ld.inc(sy.dma_start(
                        rhs[:], dram_fm(gushscr, T, t0, 512, nch=16)), 16)
                if p > 0:
                    ld.inc(sy.dma_start(part[:], dram_fm(h3part, T, t0, 512)), 16)
            blk.sync(fldr)
            toks = []
            for uc in range(NCH):
                b = prog.bank()
                prog.mm_acc(b, 512, [(wp.tile(uc, dc), rhs[:, dc]) for dc in range(n_dc_p)],
                            waits=[wp.sem.tok(), ld.tok()])
                if p == 0:
                    toks.append(prog.evac(b, "vector", lambda v, b=b, uc=uc: v.tensor_copy(
                        prog.ybuf[:, uc], b.ps[:])))
                elif p < NPASS - 1:
                    toks.append(prog.evac(b, "vector", lambda v, b=b, uc=uc: v.tensor_add(
                        prog.ybuf[:, uc], b.ps[:], part[:, uc])))
                else:
                    toks.append(prog.evac(b, "vector", lambda v, b=b, uc=uc: v.tensor_add(
                        h3t[:, uc], b.ps[:], part[:, uc])))

            def fst(sy, t0=t0, p=p, toks=tuple(toks)):
                for tk in toks:
                    wait_tok(sy, tk)
                if p < NPASS - 1:
                    st.inc(sy.dma_start(dram_fm(h3part, T, t0, 512), prog.ybuf[:]), 16)
                else:
                    st.inc(sy.dma_start(dram_fm(h3scr, T, t0, 512), h3t[:]), 16)
            blk.sync(fst)
        drain([st])
    if debug_stop == "l1_h3":
        emit_copy_scr(prog, h3scr, yscr, dt=BF16)
        return

    prog.arena_reset(full=True)
    w_lin1 = WeightFM(prog, W["L1_ffn_lin1_w"])
    w_s2a = WeightFM(prog, W["L1_ffn_swi2_w1"])
    w_s2b = WeightFM(prog, W["L1_ffn_swi2_w2"])
    rhs = prog.aalloc([128, NCH, 512], BF16)
    h4t = prog.aalloc([128, NCH, 512], BF16)
    gbuf = prog.aalloc([128, NCH, 512], F32)
    bcols2 = prog.aalloc([128, 64], F32)
    blk = prog.phase()
    w_lin1.load(prog)
    w_s2a.load(prog)
    w_s2b.load(prog)

    def fld2(gp):
        ld.inc(gp.dma_start(bcols2[:, 0:48], W["L1_bcols"][:, 0:48]), 16)
    blk.gpsimd(fld2)

    for tc in range(prog.NT):
        t0 = tc * 512
        blk = prog.phase()

        def fldr(sy, t0=t0):
            ld.inc(sy.dma_start(rhs[:], dram_fm(h3scr, T, t0, 512)), 16)
        blk.sync(fldr)
        toks1 = []
        for uc in range(NCH):
            b = prog.bank()
            prog.mm_acc(b, 512, [(w_lin1.tile(uc, dc), rhs[:, dc]) for dc in range(NCH)],
                        waits=[w_lin1.sem.tok(), ld.tok()])
            toks1.append(prog.evac(b, "scalar", lambda sc, b=b, uc=uc: sc.activation(
                h4t[:, uc], b.ps[:], AF.Identity, bias=bcols2[:, BL1 + uc:BL1 + uc + 1])))
        toks2 = []
        for uc in range(NCH):
            bg = prog.bank()
            prog.mm_acc(bg, 512, [(w_s2a.tile(uc, dc), h4t[:, dc]) for dc in range(NCH)],
                        waits=[w_s2a.sem.tok()] + toks1)
            tg = prog.evac(bg, "scalar", lambda sc, b=bg, uc=uc: sc.activation(
                gbuf[:, uc], b.ps[:], AF.Silu, bias=bcols2[:, B2A + uc:B2A + uc + 1]))
            bu = prog.bank()
            prog.mm_acc(bu, 512, [(w_s2b.tile(uc, dc), h4t[:, dc]) for dc in range(NCH)],
                        waits=[w_s2b.sem.tok()] + toks1)

            def fu(v, b=bu, uc=uc, tg=tg):
                wait_tok(v, tg)
                return v.scalar_tensor_tensor(prog.ybuf[:, uc], b.ps[:],
                                              bcols2[:, B2B + uc:B2B + uc + 1],
                                              gbuf[:, uc], ALU.add, ALU.mult)
            toks2.append(prog.evac(bu, "vector", fu))

        def fst(sy, t0=t0, toks=tuple(toks2)):
            for tk in toks:
                wait_tok(sy, tk)
            st.inc(sy.dma_start(dram_fm(yscr, T, t0, 512), prog.ybuf[:]), 16)
        blk.sync(fst)
    drain([st])
    if debug_stop == "l1_end":
        return


def emit_dump_fm(prog, get_chunk, n_chunks, dst):
    T = prog.T
    blk = prog.phase()
    dbg = prog.sem("dbg")
    cv = prog.sem("dbgcv")
    for dc in range(n_chunks):
        for tc in range(prog.NT):
            def fcv(v, dc=dc, tc=tc):
                dbg.wait(v)
                cv.inc(v.tensor_copy(prog.tscr[:], get_chunk(dc)[:, tc * 512:(tc + 1) * 512]))
            blk.vector(fcv)

            def fdbg(sy, dc=dc, tc=tc):
                cv.wait(sy)
                dbg.inc(sy.dma_start(dram_row(dst, T, dc, tc * 512, 512), prog.tscr[:]), 16)
            blk.sync(fdbg)
    prog.phase()


def emit_copy_scr(prog, src, dst, dt=BF16):
    T = prog.T
    blk = prog.phase()
    dbg = prog.sem("dbg")
    cv = prog.sem("dbgcv")
    for dc in range(NCH):
        for tc in range(prog.NT):
            def fld(sy, dc=dc, tc=tc):
                dbg.wait(sy)
                cv.inc(sy.dma_start(
                    prog.xb16[:, 0],
                    AP(src, dc * 128 * T + tc * 512, [[T, 128], [1, 512]])), 16)
            blk.sync(fld)

            def fcv(v):
                cv.wait(v)
                cv.inc(v.tensor_copy(prog.tscr[:], prog.xb16[:, 0]))
            blk.vector(fcv)

            def fstd(sy, dc=dc, tc=tc):
                cv.wait(sy)
                dbg.inc(sy.dma_start(dram_row(dst, T, dc, tc * 512, 512), prog.tscr[:]), 16)
            blk.sync(fstd)
    prog.phase()


def declare_l1_ffn(prog, yscr):
    nc = prog.nc
    W = {"yscr": yscr}
    T = prog.T
    W["L1_ffn_lin0_w"] = declare_wfm(nc, "L1_ffn_lin0_w", D, D)
    W["L1_ffn_swi1_w1"] = declare_wfm(nc, "L1_ffn_swi1_w1", D, D)
    W["L1_ffn_swi1_w2"] = declare_wfm(nc, "L1_ffn_swi1_w2", D, D)
    W["L1_ffn_lin1_w"] = declare_wfm(nc, "L1_ffn_lin1_w", D, D)
    W["L1_ffn_swi2_w1"] = declare_wfm(nc, "L1_ffn_swi2_w1", D, D)
    W["L1_ffn_swi2_w2"] = declare_wfm(nc, "L1_ffn_swi2_w2", D, D)
    W["L1_ffn_sh_w1"] = declare_wfm(nc, "L1_ffn_sh_w1", D, 2 * D)
    W["L1_ffn_sh_w3"] = declare_wfm(nc, "L1_ffn_sh_w3", D, 2 * D)
    W["L1_shw2"] = declare_wfm(nc, "L1_shw2", 2 * D, D)
    W["L1_ffn_exp_w1"] = nc.dram_tensor("L1_ffn_exp_w1", (8, 8, 128, 8, 128), F32,
                                        kind="ExternalInput")
    W["L1_ffn_exp_w3"] = nc.dram_tensor("L1_ffn_exp_w3", (8, 8, 128, 8, 128), F32,
                                        kind="ExternalInput")
    for p in range(4):
        W[f"L1_w2cat{p}"] = declare_wfm(nc, f"L1_w2cat{p}", 2 * D, D)
    W["L1_bcols"] = nc.dram_tensor("L1_bcols", (128, 64), F32, kind="ExternalInput")
    W["L1_gateT"] = nc.dram_tensor("L1_gateT", (128, 64), F32, kind="ExternalInput")
    W["csdram"] = nc.dram_tensor("l1_csdram", (8 * T,), F32, kind="Internal")
    W["h2scr"] = nc.dram_tensor("h2scr", (D * T,), BF16, kind="Internal")
    W["h3scr"] = nc.dram_tensor("h3scr", (D * T,), BF16, kind="Internal")
    W["h3part"] = nc.dram_tensor("h3part", (D * T,), F32, kind="Internal")
    W["guscr"] = nc.dram_tensor("guscr", (8 * D * T,), BF16, kind="Internal")
    W["gushscr"] = nc.dram_tensor("gushscr", (2 * D * T,), BF16, kind="Internal")
    return W


def host_cols(vecs, ncols=64):
    arr = np.zeros((128, ncols), np.float32)
    for v, vec in enumerate(vecs):
        vec = np.asarray(vec).reshape(NCH, 128)
        for uc in range(NCH):
            arr[:, v * 8 + uc] = vec[uc]
    return arr


def prep_l1_ffn(inputs):
    out = {}
    for nm in ("lin0_w", "swi1_w1", "swi1_w2", "lin1_w", "swi2_w1", "swi2_w2",
               "sh_w1", "sh_w3"):
        out[f"L1_ffn_{nm}"] = host_wfm(inputs[f"L1_ffn_{nm}"])
    out["L1_shw2"] = host_wfm(inputs["L1_ffn_sh_w2"])
    ew1 = np.asarray(inputs["L1_ffn_exp_w1"])
    ew3 = np.asarray(inputs["L1_ffn_exp_w3"])
    out["L1_ffn_exp_w1"] = np.stack([host_wfm(ew1[e]) for e in range(8)], 0)
    out["L1_ffn_exp_w3"] = np.stack([host_wfm(ew3[e]) for e in range(8)], 0)
    ew2 = np.asarray(inputs["L1_ffn_exp_w2"])
    for p in range(4):
        out[f"L1_w2cat{p}"] = host_wfm(np.concatenate([ew2[2 * p], ew2[2 * p + 1]], 0))
    out["L1_bcols"] = host_cols([inputs["L1_ffn_lin0_b"], inputs["L1_ffn_swi1_b1"],
                                 inputs["L1_ffn_swi1_b2"], inputs["L1_ffn_swi2_b1"],
                                 inputs["L1_ffn_swi2_b2"], inputs["L1_ffn_lin1_b"]])
    gt = np.asarray(inputs["L1_ffn_gate_w"]).T
    out["L1_gateT"] = np.ascontiguousarray(
        gt.reshape(NCH, 128, 8).transpose(1, 0, 2).reshape(128, 64)).astype(np.float32)
    return out



def emit_l3_ffn(prog, W, debug_stop=None):
    nc = prog.nc
    T = prog.T
    yscr = W["yscr"]
    zscr = W["zscr"]
    znscr = W["znscr"]
    guscr3 = W["guscr3"]
    cwdram = W["cwdram"]
    part3 = W["part3"]
    E, LOOPS = 8, 2
    RMS_EPS, EPS = 1e-6, 1e-9

    ld = prog.sem("l3ld")
    st = prog.sem("l3st")
    csem = prog.sem("l3csem")

    def drain(sems):
        blk = prog.phase()

        def f(sy):
            for s in sems:
                s.wait(sy)
        blk.sync(f)

    blk = prog.phase()
    cv = prog.sem("l3cv")
    for tc in range(prog.NT):
        t0 = tc * 512

        def fcv(v, t0=t0):
            st.wait(v)
            ins = None
            for dc in range(NCH):
                ins = v.tensor_copy(prog.ybuf[:, dc], prog.xn[:, dc, t0:t0 + 512])
            cv.inc(ins)
        blk.vector(fcv)

        def fst(sy, t0=t0):
            cv.wait(sy)
            st.inc(sy.dma_start(dram_fm(zscr, T, t0, 512), prog.ybuf[:]), 16)
        blk.sync(fst)
    drain([st])

    for loop in range(LOOPS):
        prog.arena_reset(full=True)
        ncols = prog.aalloc([128, 8], F32)
        routerT = prog.aalloc([128, 64], BF16)
        znt = prog.aalloc([128, NCH, 512], BF16)
        zft = prog.aalloc([128, NCH, 512], F32)
        sq3 = prog.aalloc([128, NCH, 512], BF16)
        cmb = prog.aalloc([128, 8], F32)
        escr = prog.aalloc([128, 8], F32)
        srt8 = prog.aalloc([128, 8], F32)
        scol = prog.aalloc([128, 4], F32)
        blk = prog.phase()

        def fld0(gp):
            ld.inc(gp.dma_start(ncols[:], W["L3_ncols"][:, 0:8]), 16)
            ld.inc(gp.dma_start(routerT[:], W["L3_routerT"][:]), 16)
        blk.gpsimd(fld0)

        for tc in range(prog.NT):
            t0 = tc * 512
            blk = prog.phase()
            sq = prog.sem("l3sq")
            ve = prog.sem("l3ve")

            def fdma(sy, t0=t0):
                ld.inc(sy.dma_start(zft[:], dram_fm(zscr, T, t0, 512)), 16)
            blk.sync(fdma)

            def fsq(sc):
                ld.wait(sc)
                ins = None
                for dc in range(NCH):
                    ins = sc.activation(sq3[:, dc], zft[:, dc], AF.Square)
                sq.inc(ins)
            blk.scalar(fsq)
            b2 = prog.bank()
            prog.mm_acc(b2, 512, [(prog.ones_bf[:], sq3[:, dc]) for dc in range(NCH)],
                        waits=[sq.tok()])
            tok2 = prog.evac(b2, "scalar", lambda sc, b=b2: sc.activation(
                prog.msq[:], b.ps[:], AF.Copy, scale=1.0 / D))

            def fsqr(sc):
                wait_tok(sc, tok2)
                ve.inc(sc.activation(prog.mu2[:], prog.msq[:], AF.Sqrt, bias=float(RMS_EPS)))
            blk.scalar(fsqr)

            def frec(v):
                ve.wait(v)
                ve.inc(v.reciprocal(prog.mu2[:], prog.mu2[:]))
            blk.vector(frec)

            def fap(v, t0=t0):
                ve.wait(v)
                ins = None
                for dc in range(NCH):
                    v.tensor_mul(prog.tscr[:], zft[:, dc], prog.mu2[:])
                    ins = v.tensor_scalar(znt[:, dc], prog.tscr[:],
                                          ncols[:, dc:dc + 1], None, ALU.mult)
                ve.inc(ins)
            blk.vector(fap)

            def fstz(sy, t0=t0):
                ve.wait(sy)
                st.inc(sy.dma_start(dram_fm(znscr, T, t0, 512), znt[:]), 16)
            blk.sync(fstz)

            for tt in range(4):
                p0 = t0 + tt * 128
                b = prog.bank()

                def fmm(t, b=b, tt=tt):
                    ve.wait(t)
                    b.free.wait(t)
                    ins = None
                    for dc in range(NCH):
                        ins = t.matmul(b.ps[:, 0:8], znt[:, dc, tt * 128:(tt + 1) * 128],
                                       routerT[:, dc * 8:(dc + 1) * 8],
                                       start=(dc == 0), stop=(dc == NCH - 1))
                    b.mm.inc(ins)
                blk.tensor(fmm)
                ch = prog.sem("l3r")

                def fsm(v, b=b):
                    b.mm.wait(v)
                    ins = v.reduce_max(scol[:, 0:1], b.ps[:, 0:8], AX.X, negate=True)
                    ch.inc(ins)
                    v.wait_ge(ch.h, ch.n)
                    ins = v.tensor_scalar(escr[:], b.ps[:, 0:8], scol[:, 0:1], None, ALU.add)
                    ch.inc(ins)
                    b.free.inc(v.engine_nop())
                blk.vector(fsm)

                def fexp(sc):
                    ch.wait(sc)
                    ch.inc(sc.activation(escr[:], escr[:], AF.Exp,
                                         accum_out=scol[:, 3:4]))
                blk.scalar(fexp)

                def ftop(v):
                    ch.wait(v)

                    def step(ins):
                        ch.inc(ins)
                        v.wait_ge(ch.h, ch.n)
                    step(v.reciprocal(scol[:, 1:2], scol[:, 3:4]))
                    step(v.tensor_scalar(escr[:], escr[:], scol[:, 1:2], None, ALU.mult))
                    step(v.max(out=srt8[:], in_=escr[:]))
                    step(v.reduce_sum(scol[:, 1:2], srt8[:, 0:2], AX.X))
                    step(v.tensor_scalar(scol[:, 1:2], scol[:, 1:2], float(EPS), None, ALU.add))
                    step(v.reciprocal(scol[:, 2:3], scol[:, 1:2]))
                    step(v.tensor_scalar(cmb[:], escr[:], srt8[:, 1:2], None, ALU.is_ge))
                    step(v.tensor_mul(cmb[:], cmb[:], escr[:]))
                    step(v.tensor_scalar(cmb[:], cmb[:], scol[:, 2:3], None, ALU.mult))
                blk.vector(ftop)

                bt = prog.bank()

                def ftr(t, bt=bt):
                    ch.wait(t)
                    bt.free.wait(t)
                    bt.mm.inc(t.transpose(bt.ps[0:8, 0:128], cmb[:], prog.ident_f32[:]))
                blk.tensor(ftr)
                tcs = prog.evac(bt, "vector", lambda v, bt=bt: (
                    wait_tok(v, (csem, csem.n)),
                    v.tensor_copy(prog.cst8[:], bt.ps[0:8, 0:128]))[-1])

                def fcst(sy, p0=p0, tcs=tcs):
                    wait_tok(sy, tcs)
                    csem.inc(sy.dma_start(AP(cwdram, p0, [[T, 8], [1, 128]]),
                                          prog.cst8[:]), 16)
                blk.sync(fcst)
        drain([st, csem])

        for e in range(E):
            for half in range(2):
                prog.arena_reset(full=True)
                w1h = WeightFM(prog, dram=None, dtype=BF16, n_uc=8, n_dc=8)
                w3h = WeightFM(prog, dram=None, dtype=BF16, n_uc=8, n_dc=8)
                rhs = prog.aalloc([128, NCH, 512], BF16)
                gut = prog.aalloc([128, NCH, 512], BF16)
                gbuf = prog.aalloc([128, NCH, 512], F32)
                cbc = prog.aalloc([128, 512], F32)
                blk = prog.phase()

                def fldw(gp, e=e, half=half, w1h=w1h, w3h=w3h):
                    k = 8 * 128
                    for uc in range(8):
                        w1h.sem.inc(gp.dma_start(w1h.sb[:, uc * k:(uc + 1) * k],
                                                 W["L3_ffn_w1"][e, half * 8 + uc]), 16)
                        w3h.sem.inc(gp.dma_start(w3h.sb[:, uc * k:(uc + 1) * k],
                                                 W["L3_ffn_w3"][e, half * 8 + uc]), 16)
                blk.gpsimd(fldw)

                for tc in range(prog.NT):
                    t0 = tc * 512
                    blk = prog.phase()

                    def fldr(sy, t0=t0, e=e):
                        ld.inc(sy.dma_start(rhs[:], dram_fm(znscr, T, t0, 512)), 16)
                        csg = ld
                        csg.inc(sy.dma_start(prog.rowstage[:],
                                             AP(cwdram, e * T + t0, [[512, 1], [1, 512]])), 16)
                    blk.sync(fldr)
                    bb = prog.bank()

                    def fbc(t, bb=bb):
                        ld.wait(t)
                        bb.free.wait(t)
                        bb.mm.inc(t.matmul(bb.ps[:], prog.ones_f1[:],
                                           prog.rowstage[0:1, 0:512], start=True, stop=True))
                    blk.tensor(fbc)
                    tcb = prog.evac(bb, "vector", lambda v, bb=bb: v.tensor_copy(cbc[:], bb.ps[:]))
                    toks = []
                    for uc in range(8):
                        bg = prog.bank()
                        prog.mm_acc(bg, 512, [(w1h.tile(uc, dc), rhs[:, dc]) for dc in range(NCH)],
                                    waits=[w1h.sem.tok(), ld.tok()])
                        tg = prog.evac(bg, "scalar", lambda sc, b=bg, uc=uc: sc.activation(
                            gbuf[:, uc], b.ps[:], AF.Silu))
                        bu = prog.bank()
                        prog.mm_acc(bu, 512, [(w3h.tile(uc, dc), rhs[:, dc]) for dc in range(NCH)],
                                    waits=[w3h.sem.tok(), ld.tok()])

                        def fu(v, b=bu, uc=uc, tg=tg, tcb=tcb):
                            wait_tok(v, tg)
                            wait_tok(v, tcb)
                            v.tensor_mul(gbuf[:, uc], b.ps[:], gbuf[:, uc])
                            return v.tensor_mul(gut[:, uc], gbuf[:, uc], cbc[:])
                        toks.append(prog.evac(bu, "vector", fu))

                    def fst(sy, t0=t0, half=half, toks=tuple(toks)):
                        for tk in toks:
                            wait_tok(sy, tk)
                        st.inc(sy.dma_start(
                            dram_fm(guscr3, T, t0, 512, nch=NCH, dc0=half * 8), gut[:]), 16)
                    blk.sync(fst)
                drain([st])

            prog.arena_reset(full=True)
            wp = WeightFM(prog, dram=None, dtype=BF16, n_uc=8, n_dc=16)
            rhs = prog.aalloc([128, 16, 512], BF16)
            part = prog.aalloc([128, NCH, 512], F32)
            zt = prog.aalloc([128, NCH, 512], F32)
            blk = prog.phase()

            def fldw2(gp, e=e, wp=wp):
                k = 16 * 128
                for uc in range(8):
                    wp.sem.inc(gp.dma_start(wp.sb[:, uc * k:(uc + 1) * k],
                                            W["L3_ffn_w2"][e, uc]), 16)
            blk.gpsimd(fldw2)

            for tc in range(prog.NT):
                t0 = tc * 512
                blk = prog.phase()

                def fldr(sy, t0=t0, e=e):
                    ld.inc(sy.dma_start(rhs[:], dram_fm(guscr3, T, t0, 512, nch=16)), 16)
                    if e > 0:
                        ld.inc(sy.dma_start(part[:], dram_fm(part3, T, t0, 512)), 16)
                    if e == E - 1:
                        ld.inc(sy.dma_start(zt[:], dram_fm(zscr, T, t0, 512)), 16)
                blk.sync(fldr)
                toks = []
                for uc in range(NCH):
                    b = prog.bank()
                    prog.mm_acc(b, 512, [(wp.tile(uc, dc), rhs[:, dc]) for dc in range(16)],
                                waits=[wp.sem.tok(), ld.tok()])
                    if e == 0:
                        toks.append(prog.evac(b, "vector", lambda v, b=b, uc=uc: v.tensor_copy(
                            prog.ybuf[:, uc], b.ps[:])))
                    elif e < E - 1:
                        toks.append(prog.evac(b, "vector", lambda v, b=b, uc=uc: v.tensor_add(
                            prog.ybuf[:, uc], b.ps[:], part[:, uc])))
                    else:
                        def ffin(v, b=b, uc=uc):
                            v.tensor_add(prog.ybuf[:, uc], b.ps[:], part[:, uc])
                            return v.tensor_add(prog.ybuf[:, uc], prog.ybuf[:, uc], zt[:, uc])
                        toks.append(prog.evac(b, "vector", ffin))

                def fst(sy, t0=t0, e=e, toks=tuple(toks)):
                    for tk in toks:
                        wait_tok(sy, tk)
                    if e < E - 1:
                        st.inc(sy.dma_start(dram_fm(part3, T, t0, 512), prog.ybuf[:]), 16)
                    else:
                        st.inc(sy.dma_start(dram_fm(zscr, T, t0, 512), prog.ybuf[:]), 16)
                blk.sync(fst)
            drain([st])

    blk = prog.phase()
    fin = prog.sem("l3fin")
    big = [[D * T // 128, 128], [1, D * T // 128]]

    def ffin2(sy):
        fin.inc(sy.dma_start(AP(yscr, 0, big), AP(zscr, 0, big)), 16)
        fin.wait(sy)
    blk.sync(ffin2)
    prog.phase()


def declare_l3_ffn(prog, yscr):
    nc = prog.nc
    T = prog.T
    W = {"yscr": yscr}
    W["L3_ffn_w1"] = nc.dram_tensor("L3_ffn_w1", (8, 16, 128, 8, 128), F32,
                                    kind="ExternalInput")
    W["L3_ffn_w3"] = nc.dram_tensor("L3_ffn_w3", (8, 16, 128, 8, 128), F32,
                                    kind="ExternalInput")
    W["L3_ffn_w2"] = nc.dram_tensor("L3_ffn_w2", (8, 8, 128, 16, 128), F32,
                                    kind="ExternalInput")
    W["L3_ncols"] = nc.dram_tensor("L3_ncols", (128, 8), F32, kind="ExternalInput")
    W["L3_routerT"] = nc.dram_tensor("L3_routerT", (128, 64), F32, kind="ExternalInput")
    W["zscr"] = nc.dram_tensor("l3_zscr", (D * T,), F32, kind="Internal")
    W["znscr"] = nc.dram_tensor("l3_znscr", (D * T,), BF16, kind="Internal")
    W["guscr3"] = nc.dram_tensor("l3_guscr", (2 * D * T,), BF16, kind="Internal")
    W["cwdram"] = nc.dram_tensor("l3_cwdram", (8 * T,), F32, kind="Internal")
    W["part3"] = nc.dram_tensor("l3_part", (D * T,), F32, kind="Internal")
    return W


def prep_l3_ffn(inputs):
    out = {}
    w1 = np.asarray(inputs["L3_ffn_w1"])
    w3 = np.asarray(inputs["L3_ffn_w3"])
    w2 = np.asarray(inputs["L3_ffn_w2"])
    out["L3_ffn_w1"] = np.stack([host_wfm(w1[e]) for e in range(8)], 0)
    out["L3_ffn_w3"] = np.stack([host_wfm(w3[e]) for e in range(8)], 0)
    out["L3_ffn_w2"] = np.stack([host_wfm(w2[e]) for e in range(8)], 0)
    nw = np.asarray(inputs["L3_ffn_norm_w"]).reshape(NCH, 128)
    out["L3_ncols"] = np.ascontiguousarray(nw.T).astype(np.float32)
    rt = np.asarray(inputs["L3_ffn_router_w"])
    out["L3_routerT"] = np.ascontiguousarray(
        rt.reshape(NCH, 128, 8).transpose(1, 0, 2).reshape(128, 64)).astype(np.float32)
    return out



def emit_l3_attn(prog, W, debug_stop=None):
    nc = prog.nc
    T = prog.T
    yscr = W["yscr"]
    NH, HD = 16, 64
    scale = 1.0 / math.sqrt(HD)
    ld = prog.sem("laald")
    st = prog.sem("laast")
    vdram = W["l3a_vdram"]

    prog.arena_reset(full=True)
    cqT = prog.aalloc([128, 4, T], BF16)
    ckvL = prog.aalloc([128, T], BF16)
    ckvH = prog.aalloc([128, T], BF16)
    MARK = prog.arena_mark()

    w_cq = WeightFM(prog, W["L3a_wq_d"])
    w_kv = WeightFM(prog, W["L3a_wkv_d"])
    bc = prog.aalloc([128, 16], F32)
    blk = prog.phase()
    w_cq.load(prog)
    w_kv.load(prog)

    def fldb(gp):
        ld.inc(gp.dma_start(bc[:], W["L3a_bc"][:]), 16)
        gp.memset(ckvH[:], 0.0)
    blk.gpsimd(fldb)
    ck_toks = []
    for tc in range(prog.NT):
        t0 = tc * 512
        for uc in range(4):
            b = prog.bank()
            prog.mm_acc(b, 512, [(w_cq.tile(uc, dc), prog.xn[:, dc, t0:t0 + 512])
                                 for dc in range(NCH)],
                        waits=[w_cq.sem.tok(), ld.tok()])
            prog.evac(b, "scalar", lambda sc, b=b, uc=uc, t0=t0: sc.activation(
                cqT[:, uc, t0:t0 + 512], b.ps[:], AF.Identity, bias=bc[:, uc:uc + 1]))
        b = prog.bank()
        prog.mm_acc(b, 512, [(w_kv.tile(0, dc), prog.xn[:, dc, t0:t0 + 512])
                             for dc in range(NCH)],
                    waits=[w_kv.sem.tok(), ld.tok()])
        ck_toks.append(prog.evac(b, "scalar", lambda sc, b=b, t0=t0: sc.activation(
            ckvL[:, t0:t0 + 512], b.ps[:], AF.Identity, bias=bc[:, 4:5])))
    blk = prog.phase()
    ckh = prog.sem("lackh")

    def fckh(sy):
        ckh.inc(sy.dma_start(ckvH[64:128, :], ckvL[0:64, :]), 16)
    blk.sync(fckh)
    if debug_stop == "l3a_cq":
        emit_dump_fm(prog, lambda dc: cqT[:, dc], 4, W["yscr"])
        return
    if debug_stop == "l3a_ckv":
        emit_dump_fm(prog, lambda dc: ckvL[:] if dc == 0 else ckvH[:], 2, W["yscr"])
        return

    prog.arena_reset_to(MARK)
    w_vu = WeightFM(prog, W["L3a_wv_u"])
    vstage = prog.aalloc([128, NH, HD + 1], BF16)
    bvu_t = prog.aalloc([128, NH * HD], F32)
    blk = prog.phase()
    w_vu.load(prog)

    mss = prog.sem("lams")

    def fldbv(gp):
        ld.inc(gp.dma_start(bvu_t[:], W["L3a_bvu_t"][:]), 16)
        mss.inc(gp.memset(vstage[:, :, HD:HD + 1], 1.0))
    blk.gpsimd(fldbv)
    for stt in range(prog.NTT):
        s0 = stt * 128
        blk = prog.phase()
        toks = []
        for g in range(2):
            b = prog.bank()

            def fmm(t, b=b, g=g, s0=s0):
                w_vu.sem.wait(t)
                ld.wait(t)
                b.free.wait(t)
                b.mm.inc(t.matmul(b.ps[:], ckvL[:, s0:s0 + 128],
                                  w_vu.sb[:, g * 512:(g + 1) * 512],
                                  start=True, stop=True))
            blk.tensor(fmm)

            def fev(v, b=b, g=g):
                ins = None
                for hh in range(8):
                    h = g * 8 + hh
                    ins = v.tensor_add(vstage[:, h, 0:HD],
                                       b.ps[:, hh * 64:(hh + 1) * 64],
                                       bvu_t[:, h * 64:(h + 1) * 64])
                return ins
            toks.append(prog.evac(b, "vector", fev))

        def fstv(sy, stt=stt, toks=tuple(toks)):
            for tk in toks:
                wait_tok(sy, tk)
            mss.wait(sy)
            st.inc(sy.dma_start(
                AP(vdram, stt * NH * 128 * (HD + 1),
                   [[HD + 1, 128], [128 * (HD + 1), NH], [1, HD + 1]]),
                vstage[:]), 16)
        blk.sync(fstv)
    blk = prog.phase()

    def fw(sy):
        st.wait(sy)
    blk.sync(fw)

    prog.arena_reset_to(MARK)
    w_qk = WeightFM(prog, W["L3a_w_qk"])
    w_o = WeightFM(prog, W["L3a_w_proj"])
    qtc = prog.aalloc([128, NCH, 512], BF16)
    vh = [prog.aalloc([128, prog.NTT, HD + 1], BF16) for _ in range(2)]
    eT = [prog.aalloc([128, 512], BF16) for _ in range(2)]
    ctxn = prog.aalloc([128, 512], BF16)
    ctxcat = prog.aalloc([128, NCH, 512], BF16)
    rcol = prog.aalloc([128, 4], F32)
    bo = prog.aalloc([128, 8], F32)
    bq = prog.aalloc([128, 8], F32)
    blk = prog.phase()
    w_qk.load(prog)
    w_o.load(prog)

    def fldo(gp):
        ld.inc(gp.dma_start(bo[:], W["L3a_bo"][:]), 16)
        ld.inc(gp.dma_start(bq[:], W["L3a_bq"][:]), 16)
    blk.gpsimd(fldo)

    lvh = prog.sem("lavh")
    etf = prog.sem("laet")
    for tc in range(prog.NT):
        t0 = tc * 512
        blk = prog.phase()
        qtoks = []
        for uc in range(NCH):
            b = prog.bank()
            prog.mm_acc(b, 512, [(w_qk.tile(uc, dc), cqT[:, dc, t0:t0 + 512])
                                 for dc in range(4)],
                        waits=[w_qk.sem.tok(), ld.tok(), ckh.tok()])
            qtoks.append(prog.evac(b, "scalar", lambda sc, b=b, uc=uc: sc.activation(
                qtc[:, uc], b.ps[:], AF.Identity, bias=bq[:, uc:uc + 1])))
        ctok = prog.sem("lact")
        trs = prog.sem("latr")
        for h in range(NH):
            uc = h // 2
            ck = ckvL if h % 2 == 0 else ckvH
            vslot = h % 2

            def fldvh(sy, h=h, vslot=vslot):
                if h >= 2:
                    ctok.wait(sy)
                lvh.inc(sy.dma_start(
                    vh[vslot][:],
                    AP(vdram, h * 128 * (HD + 1),
                       [[HD + 1, 128], [NH * 128 * (HD + 1), prog.NTT], [1, HD + 1]])), 16)
            blk.sync(fldvh)
            ltok = lvh.tok()
            cbanks = prog.reserve_banks(4)
            etoks = [None, None]
            for si in range(prog.NTT):
                bs = prog.bank()

                def fs(t, bs=bs, si=si, t0=t0, ck=ck, uc=uc, qtoks=tuple(qtoks)):
                    for tk in qtoks:
                        wait_tok(t, tk)
                    bs.free.wait(t)
                    bs.mm.inc(t.matmul(bs.ps[:], ck[:, si * 128:(si + 1) * 128],
                                       qtc[:, uc], start=True, stop=True))
                blk.tensor(fs)

                def fexp(sc, bs=bs, si=si, et=etoks[si % 2]):
                    bs.mm.wait(sc)
                    if et is not None:
                        wait_tok(sc, et)
                    ins = sc.activation(eT[si % 2][:], bs.ps[:], AF.Exp,
                                        scale=float(scale))
                    bs.free.inc(ins)
                blk.scalar(fexp)
                for tt in range(4):
                    cb = cbanks[tt]

                    def fc(t, cb=cb, si=si, tt=tt, ltok=ltok, bs=bs, vslot=vslot):
                        wait_tok(t, (bs.free, bs.free.n))
                        if si == 0:
                            wait_tok(t, ltok)
                            cb.free.wait(t)
                        ins = t.matmul(cb.ps[:, 0:HD + 1],
                                       eT[si % 2][:, tt * 128:(tt + 1) * 128],
                                       vh[vslot][:, si, :], start=(si == 0),
                                       stop=(si == prog.NTT - 1))
                        if tt == 3:
                            etf.inc(ins)
                    blk.tensor(fc)
                etoks[si % 2] = etf.tok()
            for tt in range(4):
                cb = cbanks[tt]

                def fnorm(v, cb=cb, tt=tt, fin=etf.tok()):
                    wait_tok(v, fin)
                    ins = v.reciprocal(rcol[:, tt:tt + 1], cb.ps[:, HD:HD + 1])
                    ctok.inc(ins)
                    v.wait_ge(ctok.h, ctok.n)
                    ins = v.tensor_scalar(ctxn[:, tt * 128:tt * 128 + HD],
                                          cb.ps[:, 0:HD], rcol[:, tt:tt + 1],
                                          None, ALU.mult)
                    ctok.inc(ins)
                    cb.free.inc(v.engine_nop())
                blk.vector(fnorm)
            prog.release_banks(cbanks)
            sub = (h % 2) * 64
            for tt in range(4):
                bt = prog.bank()

                def ftr(t, bt=bt, tt=tt, ctk=ctok.tok()):
                    wait_tok(t, ctk)
                    bt.free.wait(t)
                    bt.mm.inc(t.transpose(
                        bt.ps[:].bitcast(BF16)[0:64, 0:128],
                        ctxn[:, tt * 128:tt * 128 + HD], prog.ident[:]))
                blk.tensor(ftr)

                def fte(v, bt=bt, tt=tt, uc=uc, sub=sub):
                    bt.mm.wait(v)
                    ins = v.tensor_copy(ctxcat[:, uc, tt * 128:(tt + 1) * 128][sub:sub + 64, :],
                                        bt.ps[:].bitcast(BF16)[0:64, 0:128])
                    trs.inc(ins)
                    bt.free.inc(v.engine_nop())
                blk.vector(fte)
        st_before = st.n
        otoks = []
        for uc in range(NCH):
            b = prog.bank()
            prog.mm_acc(b, 512, [(w_o.tile(uc, dc), ctxcat[:, dc]) for dc in range(NCH)],
                        waits=[w_o.sem.tok(), trs.tok()])

            def fev(sc, b=b, uc=uc, st_before=st_before):
                sc.wait_ge(st.h, st_before)
                return sc.activation(prog.ybuf[:, uc], b.ps[:], AF.Identity,
                                     bias=bo[:, uc:uc + 1])
            otoks.append(prog.evac(b, "scalar", fev))

        def fsty(sy, t0=t0, otoks=tuple(otoks)):
            for tk in otoks:
                wait_tok(sy, tk)
            st.inc(sy.dma_start(dram_fm(yscr, T, t0, 512), prog.ybuf[:]), 16)
        blk.sync(fsty)
    blk = prog.phase()

    def fw2(sy):
        st.wait(sy)
    blk.sync(fw2)
    prog.phase()


def declare_l3_attn(prog, yscr):
    nc = prog.nc
    T = prog.T
    W = {"yscr": yscr}
    W["L3a_wq_d"] = declare_wfm(nc, "L3a_wq_d", D, 512)
    W["L3a_wkv_d"] = declare_wfm(nc, "L3a_wkv_d", D, 128)
    W["L3a_w_qk"] = declare_wfm(nc, "L3a_w_qk", 512, D)
    W["L3a_wv_u"] = declare_wfm(nc, "L3a_wv_u", 128, D)
    W["L3a_w_proj"] = declare_wfm(nc, "L3a_w_proj", D, D)
    W["L3a_bc"] = nc.dram_tensor("L3a_bc", (128, 16), F32, kind="ExternalInput")
    W["L3a_bq"] = nc.dram_tensor("L3a_bq", (128, 8), F32, kind="ExternalInput")
    W["L3a_bo"] = nc.dram_tensor("L3a_bo", (128, 8), F32, kind="ExternalInput")
    W["L3a_bvu_t"] = nc.dram_tensor("L3a_bvu_t", (128, 1024), F32, kind="ExternalInput")
    W["l3a_vdram"] = nc.dram_tensor("l3a_vdram", (T * 16 * 65,), BF16, kind="Internal")
    return W


def prep_l3_attn(inputs):
    out = {}
    out["L3a_wq_d"] = host_wfm(np.asarray(inputs["L3_attn_wq_d"]).T)
    wkv = np.asarray(inputs["L3_attn_wkv_d"]).T
    out["L3a_wkv_d"] = host_wfm(np.concatenate(
        [wkv, np.zeros((1024, 64), np.float32)], 1))
    out["L3a_w_qk"] = host_wfm(np.asarray(inputs["L3_attn_w_qk"]).T)
    wvu = np.asarray(inputs["L3_attn_wv_u"]).T
    out["L3a_wv_u"] = host_wfm(np.concatenate(
        [wvu, np.zeros((64, 1024), np.float32)], 0))
    out["L3a_w_proj"] = host_wfm(np.asarray(inputs["L3_attn_w_proj"]).T)
    bc = np.zeros((128, 16), np.float32)
    bqd = np.asarray(inputs["L3_attn_bq_d"]).reshape(4, 128)
    for uc in range(4):
        bc[:, uc] = bqd[uc]
    bkv = np.asarray(inputs["L3_attn_bkv_d"])
    bc[0:64, 4] = bkv
    out["L3a_bc"] = bc
    out["L3a_bq"] = np.ascontiguousarray(
        np.asarray(inputs["L3_attn_b_qk"]).reshape(8, 128).T).astype(np.float32)
    out["L3a_bo"] = np.ascontiguousarray(np.asarray(inputs["L3_attn_b_proj"]).reshape(8, 128).T)
    out["L3a_bvu_t"] = np.broadcast_to(np.asarray(inputs["L3_attn_bv_u"])[None, :],
                                       (128, 1024)).copy()
    return out



def emit_l0_attn(prog, W, debug_stop=None):
    nc = prog.nc
    T = prog.T
    yscr = W["yscr"]
    H, M, DK = 4, 64, 256
    scale = DK ** (-0.5)
    CLAMP = 32.0
    ld = prog.sem("l0ald")
    st = prog.sem("l0ast")
    qD, kD, vD, gD, oD, ofD = (W["l0a_q"], W["l0a_k"], W["l0a_v"], W["l0a_g"],
                               W["l0a_o"], W["l0a_of"])

    prog.arena_reset(full=True)
    midk = prog.aalloc([128, T], BF16)
    midv = prog.aalloc([128, T], BF16)
    w2k = prog.aalloc([128, 256], BF16)
    w2v = prog.aalloc([128, 256], BF16)
    b2t = prog.aalloc([128, 512], F32)
    gnc = prog.aalloc([128, 2], F32)
    onesT = prog.aalloc([128, T], BF16)
    MARK = prog.arena_mark()

    w1k = WeightFM(prog, W["L0a_skw1"])
    w1v = WeightFM(prog, W["L0a_svw1"])
    blk = prog.phase()
    w1k.load(prog)
    w1v.load(prog)

    def fldc(gp):
        ld.inc(gp.dma_start(w2k[:], W["L0a_skw2"][:]), 16)
        ld.inc(gp.dma_start(w2v[:], W["L0a_svw2"][:]), 16)
        ld.inc(gp.dma_start(b2t[:], W["L0a_b2t"][:]), 16)
        ld.inc(gp.dma_start(gnc[:], W["L0a_gnc"][:]), 16)
        ld.inc(gp.memset(onesT[:], 1.0), 1)
    blk.gpsimd(fldc)
    for tc in range(prog.NT):
        t0 = tc * 512
        for (w1, mid) in ((w1k, midk), (w1v, midv)):
            b = prog.bank()
            prog.mm_acc(b, 512, [(w1.tile(0, dc), prog.xn[:, dc, t0:t0 + 512])
                                 for dc in range(NCH)], waits=[w1.sem.tok()])
            prog.evac(b, "vector", lambda v, b=b, mid=mid, t0=t0: v.tensor_copy(
                mid[0:16, t0:t0 + 512], b.ps[0:16, :]))
    prog.phase()

    for h in range(H):
        prog.arena_reset_to(MARK)
        wq = WeightFM(prog, dram=None, dtype=BF16, n_uc=2, n_dc=8)
        wk_ = WeightFM(prog, dram=None, dtype=BF16, n_uc=2, n_dc=8)
        wv_ = WeightFM(prog, dram=None, dtype=BF16, n_uc=2, n_dc=8)
        wg_ = WeightFM(prog, dram=None, dtype=BF16, n_uc=2, n_dc=8)
        cst = prog.aalloc([128, 512], F32)
        snt = prog.aalloc([128, 512], F32)
        rt1 = prog.aalloc([128, 512], F32)
        qkst = prog.aalloc([128, 2, 512], BF16)
        gst = prog.aalloc([128, 512], BF16)
        vst = prog.aalloc([128, 256], BF16)
        blk = prog.phase()

        def fldw(gp, h=h, wq=wq, wk_=wk_, wv_=wv_, wg_=wg_):
            k = 8 * 128
            for uc in range(2):
                for (wfm, wd) in ((wq, W["L0a_Wq"]), (wk_, W["L0a_Wk"]),
                                  (wv_, W["L0a_Wv"]), (wg_, W["L0a_Wg"])):
                    wfm.sem.inc(gp.dma_start(wfm.sb[:, uc * k:(uc + 1) * k],
                                             wd[h * 2 + uc]), 16)
        blk.gpsimd(fldw)
        rp = prog.sem("l0arp")
        for tc in range(prog.NT):
            t0 = tc * 512
            blk = prog.phase()

            def fldcs(sy, t0=t0):
                rp.inc(sy.dma_start(cst[:], AP(W["L0a_cos"], t0, [[T, 128], [1, 512]])), 16)
                rp.inc(sy.dma_start(snt[:], AP(W["L0a_sin"], t0, [[T, 128], [1, 512]])), 16)
            blk.sync(fldcs)
            for (wfm, dst) in ((wq, qD), (wk_, kD)):
                b0 = prog.bank()
                b1 = prog.bank()
                prog.mm_acc(b0, 512, [(wfm.tile(0, dc), prog.xn[:, dc, t0:t0 + 512])
                                      for dc in range(NCH)], waits=[wfm.sem.tok()])
                prog.mm_acc(b1, 512, [(wfm.tile(1, dc), prog.xn[:, dc, t0:t0 + 512])
                                      for dc in range(NCH)], waits=[wfm.sem.tok()])
                rtok = prog.sem("l0art")
                st_b = st.n

                def frope(v, b0=b0, b1=b1, st_b=st_b):
                    b0.mm.wait(v)
                    b1.mm.wait(v)
                    rp.wait(v)
                    v.wait_ge(st.h, st_b)
                    v.tensor_mul(rt1[:], b0.ps[:], cst[:])
                    v.tensor_mul(prog.tscr[:], b1.ps[:], snt[:])
                    v.tensor_sub(qkst[:, 0, :], rt1[:], prog.tscr[:])
                    v.tensor_mul(rt1[:], b1.ps[:], cst[:])
                    v.tensor_mul(prog.tscr[:], b0.ps[:], snt[:])
                    ins = v.tensor_add(qkst[:, 1, :], rt1[:], prog.tscr[:])
                    rtok.inc(ins)
                    b0.free.inc(v.engine_nop())
                    b1.free.inc(v.engine_nop())
                blk.vector(frope)

                def fstq(sy, dst=dst, t0=t0, rtok=rtok):
                    rtok.wait(sy)
                    st.inc(sy.dma_start(
                        AP(dst, t0, [[T, 128], [128 * T, 2], [1, 512]]), qkst[:]), 16)
                blk.sync(fstq)
        blk = prog.phase()
        for kt in range(prog.NTT):
            k0 = kt * 128
            b = prog.bank()
            st_b = st.n

            def fvm(t, b=b, k0=k0, wv_=wv_):
                wv_.sem.wait(t)
                b.free.wait(t)
                ins = None
                for dc in range(NCH):
                    ins = t.matmul(b.ps[:, 0:128], prog.xn[:, dc, k0:k0 + 128],
                                   wv_.tile(0, dc), start=(dc == 0), stop=False
                                   if dc < NCH - 1 else True)
                for dc in range(NCH):
                    ins = t.matmul(b.ps[:, 128:256], prog.xn[:, dc, k0:k0 + 128],
                                   wv_.tile(1, dc), start=(dc == 0), stop=(dc == NCH - 1),
                                   skip_group_check=True)
                b.mm.inc(ins)
            blk.tensor(fvm)

            def fve(v, b=b, st_b=st_b):
                v.wait_ge(st.h, st_b)
                return v.tensor_copy(vst[:], b.ps[:, 0:256])
            tkv = prog.evac(b, "vector", fve)

            def fstv(sy, kt=kt, tkv=tkv):
                wait_tok(sy, tkv)
                st.inc(sy.dma_start(
                    AP(vD, kt * 128 * 256, [[256, 128], [1, 256]]), vst[:]), 16)
            blk.sync(fstv)
        for tc in range(prog.NT):
            t0 = tc * 512
            for uc in range(2):
                b = prog.bank()
                st_b = st.n
                prog.mm_acc(b, 512, [(wg_.tile(uc, dc), prog.xn[:, dc, t0:t0 + 512])
                                     for dc in range(NCH)], waits=[wg_.sem.tok()])

                def fge(sc, b=b, st_b=st_b):
                    sc.wait_ge(st.h, st_b)
                    return sc.activation(gst[:], b.ps[:], AF.Silu)
                tkg = prog.evac(b, "scalar", fge)

                def fstg(sy, uc=uc, t0=t0, tkg=tkg):
                    wait_tok(sy, tkg)
                    st.inc(sy.dma_start(
                        AP(gD, uc * 128 * T + t0, [[T, 128], [1, 512]]), gst[:]), 16)
                blk.sync(fstg)

        prog.arena_reset_to(MARK)
        wkt = prog.aalloc([128, prog.NTT, M], BF16)
        zkt = prog.aalloc([128, prog.NTT, M], BF16)
        zvt = prog.aalloc([128, prog.NTT, M], BF16)
        wvT = prog.aalloc([128, T], BF16)
        MARK2 = prog.arena_mark()
        wvt = prog.aalloc([128, prog.NTT, M], BF16)
        wkT = prog.aalloc([128, T], BF16)
        zkT = prog.aalloc([128, T], F32)
        zvT = prog.aalloc([128, T], F32)
        sks = prog.aalloc([128, M], F32)
        blk = prog.phase()
        h2s = prog.sem("l0ah2")
        for (mid, w2, boff, wtok, wT) in ((midk, w2k, 0, wkt, wkT),
                                          (midv, w2v, 256, wvt, wvT)):
            for tt in range(prog.NTT):
                p0 = tt * 128
                b = prog.bank()

                def fmm(t, b=b, mid=mid, w2=w2, p0=p0, h=h):
                    ld.wait(t)
                    b.free.wait(t)
                    b.mm.inc(t.matmul(b.ps[:, 0:M], mid[0:16, p0:p0 + 128],
                                      w2[0:16, h * M:(h + 1) * M],
                                      start=True, stop=True))
                blk.tensor(fmm)

                def fcl(v, b=b, boff=boff, h=h):
                    b.mm.wait(v)
                    v.tensor_add(sks[:], b.ps[:, 0:M],
                                 b2t[:, boff + h * M:boff + (h + 1) * M])
                    ins = v.tensor_scalar(sks[:], sks[:], CLAMP, -CLAMP,
                                          ALU.min, ALU.max)
                    h2s.inc(ins)
                    b.free.inc(v.engine_nop())
                blk.vector(fcl)

                def fex(sc, wtok=wtok, tt=tt):
                    h2s.wait(sc)
                    h2s.inc(sc.activation(wtok[:, tt, :], sks[:], AF.Exp))
                blk.scalar(fex)
                bt = prog.bank()

                def ftr(t, bt=bt, wtok=wtok, tt=tt):
                    h2s.wait(t)
                    bt.free.wait(t)
                    bt.mm.inc(t.transpose(bt.ps[:].bitcast(BF16)[0:M, 0:128],
                                          wtok[:, tt, :], prog.ident[:]))
                blk.tensor(ftr)
                prog.evac(bt, "vector", lambda v, bt=bt, wT=wT, p0=p0: v.tensor_copy(
                    wT[0:M, p0:p0 + 128], bt.ps[:].bitcast(BF16)[0:M, 0:128]))
        blk = prog.phase()
        scn = prog.sem("l0ascn")

        def fscan(v):
            ins = v.tensor_tensor_scan(zkT[0:M, :], onesT[0:M, :], wkT[0:M, :],
                                       0.0, ALU.mult, ALU.add)
            scn.inc(ins)
            v.wait_ge(scn.h, scn.n)
            ins = v.tensor_tensor_scan(zvT[0:M, :], onesT[0:M, :], wvT[0:M, :],
                                       0.0, ALU.mult, ALU.add)
            scn.inc(ins)
        blk.vector(fscan)
        blk = prog.phase()
        for (zT, ztok) in ((zkT, zkt), (zvT, zvt)):
            for tt in range(prog.NTT):
                p0 = tt * 128
                bt = prog.bank()

                def ftr2(t, bt=bt, zT=zT, p0=p0):
                    scn.wait(t)
                    bt.free.wait(t)
                    bt.mm.inc(t.transpose(bt.ps[:, 0:M], zT[0:M, p0:p0 + 128],
                                          prog.ident_f32[0:M, 0:M]))
                blk.tensor(ftr2)
                prog.evac(bt, "vector", lambda v, bt=bt, ztok=ztok, tt=tt: v.tensor_copy(
                    ztok[:, tt, :], bt.ps[:, 0:M]))
        blk = prog.phase()
        rcz = prog.sem("l0arcz")

        def finv(v):
            with nc.allow_low_precision("bf16 1/z within tolerance"):
                ins = v.reciprocal(zkt[:].rearrange("p a b -> p (a b)"),
                                   zkt[:].rearrange("p a b -> p (a b)"))
                rcz.inc(ins)
                v.wait_ge(rcz.h, rcz.n)
                ins = v.reciprocal(zvt[:].rearrange("p a b -> p (a b)"),
                                   zvt[:].rearrange("p a b -> p (a b)"))
                rcz.inc(ins)
        blk.vector(finv)
        prog.phase()

        prog.arena_reset_to(MARK2)
        ssb = prog.aalloc([128, 2, 512], BF16)
        qsl = prog.aalloc([128, 2, 512], BF16)
        ktl = prog.aalloc([128, 2, T], BF16)
        vtl = prog.aalloc([128, prog.NTT, 256], BF16)
        pzT = prog.aalloc([128, 512], BF16)
        pzs = prog.aalloc([128, M], BF16)
        ost = prog.aalloc([128, 2, 512], BF16)
        ncol = prog.aalloc([128, 8], F32)
        asb = prog.aalloc([128, 512], BF16)
        sks2 = prog.aalloc([128, M], F32)
        blk = prog.phase()
        lda = prog.sem("l0alda")

        def fldk(sy, h=h):
            st.wait(sy)
            lda.inc(sy.dma_start(ktl[:], AP(kD, 0, [[T, 128], [128 * T, 2], [1, T]])), 16)
            lda.inc(sy.dma_start(vtl[:], AP(vD, 0, [[256, 128], [128 * 256, prog.NTT], [1, 256]])), 16)
        blk.sync(fldk)
        amm = prog.sem("l0aamm")
        smsk = prog.sem("l0asm")
        for tc in range(prog.NT):
            t0 = tc * 512
            blk = prog.phase()

            def fldq(sy, t0=t0):
                lda.inc(sy.dma_start(qsl[:], AP(qD, t0, [[T, 128], [128 * T, 2], [1, 512]])), 16)
            blk.sync(fldq)
            nkt = 4 * tc + 4
            lbanks = prog.reserve_banks(4)
            for kt in range(nkt):
                b = prog.bank()
                sslot = kt % 2
                prev_log = amm.tok()
                prog.mm_acc(b, 512, [(ktl[:, c, kt * 128:(kt + 1) * 128], qsl[:, c, :])
                                     for c in range(2)], waits=[lda.tok()])

                def fse(v, b=b, sslot=sslot, prev_log=prev_log):
                    b.mm.wait(v)
                    wait_tok(v, prev_log)
                    ins = v.tensor_copy(ssb[:, sslot, :], b.ps[:])
                    smsk.inc(ins)
                    b.free.inc(v.engine_nop())
                blk.vector(fse)
                if kt >= 4 * tc:
                    def fmask(gp, kt=kt, t0=t0, sslot=sslot, n=smsk.n):
                        gp.wait_ge(smsk.h, n)
                        smsk.inc(gp.affine_select(
                            ssb[:, sslot, :], ssb[:, sslot, :], [[1, 512]], ALU.is_ge,
                            0.0, base=t0 - kt * 128, channel_multiplier=-1))
                    blk.gpsimd(fmask)
                stok = smsk.tok()
                for tt in range(4):
                    lb = lbanks[tt]

                    def flog(t, lb=lb, kt=kt, tt=tt, sslot=sslot, stok=stok, nkt=nkt):
                        wait_tok(t, stok)
                        if kt == 0:
                            lb.free.wait(t)
                        ins = t.matmul(lb.ps[:, 0:M],
                                       ssb[:, sslot, tt * 128:(tt + 1) * 128],
                                       wkt[:, kt, :], start=(kt == 0),
                                       stop=(kt == nkt - 1))
                        if tt == 3:
                            amm.inc(ins)
                    blk.tensor(flog)
            lg = prog.sem("l0alg")
            prev_tr = None
            pz_toks = []
            for tt in range(4):
                gt = 4 * tc + tt
                lb = lbanks[tt]

                def fsm(v, lb=lb, gt=gt, fin=amm.tok(), prev_tr=prev_tr):
                    wait_tok(v, fin)
                    if prev_tr is not None:
                        wait_tok(v, prev_tr)

                    def step(ins):
                        lg.inc(ins)
                        v.wait_ge(lg.h, lg.n)
                    step(v.tensor_tensor(sks2[:], lb.ps[:, 0:M], zkt[:, gt, :], ALU.mult))
                    step(v.reduce_max(ncol[:, 0:1], sks2[:], AX.X))
                    step(v.tensor_scalar_mul(ncol[:, 1:2], ncol[:, 0:1], -float(scale)))
                    lb.free.inc(v.engine_nop())
                blk.vector(fsm)

                def fexp(sc):
                    lg.wait(sc)
                    lg.inc(sc.activation(sks2[:], sks2[:], AF.Exp, scale=float(scale),
                                         bias=ncol[:, 1:2], accum_out=ncol[:, 2:3]))
                blk.scalar(fexp)

                def fpz(v, gt=gt):
                    lg.wait(v)

                    def step(ins):
                        lg.inc(ins)
                        v.wait_ge(lg.h, lg.n)
                    step(v.reciprocal(ncol[:, 3:4], ncol[:, 2:3]))
                    step(v.tensor_scalar(sks2[:], sks2[:], ncol[:, 3:4], None, ALU.mult))
                    step(v.tensor_tensor(pzs[:], sks2[:], zvt[:, gt, :], ALU.mult))
                blk.vector(fpz)
                bt = prog.bank()

                def ftr3(t, bt=bt, n=lg.n):
                    t.wait_ge(lg.h, n)
                    bt.free.wait(t)
                    bt.mm.inc(t.transpose(bt.ps[:].bitcast(BF16)[0:M, 0:128],
                                          pzs[:], prog.ident[:]))
                blk.tensor(ftr3)
                prev_tr = bt.mm.tok()
                pz_toks.append(prog.evac(bt, "vector", lambda v, bt=bt, tt=tt: v.tensor_copy(
                    pzT[0:M, tt * 128:(tt + 1) * 128],
                    bt.ps[:].bitcast(BF16)[0:M, 0:128])))
            prog.release_banks(lbanks)
            obanks = prog.reserve_banks(2)
            for kt in range(nkt):
                ba = prog.bank()

                def fa(t, ba=ba, kt=kt, pz_toks=tuple(pz_toks)):
                    for tk in pz_toks:
                        wait_tok(t, tk)
                    ba.free.wait(t)
                    ba.mm.inc(t.matmul(ba.ps[:], wvT[0:M, kt * 128:(kt + 1) * 128],
                                       pzT[0:M, :], start=True, stop=True))
                blk.tensor(fa)
                prev_o = amm.tok()

                def fae(v, ba=ba, prev_o=prev_o):
                    ba.mm.wait(v)
                    wait_tok(v, prev_o)
                    ins = v.tensor_copy(asb[:], ba.ps[:])
                    smsk.inc(ins)
                    ba.free.inc(v.engine_nop())
                blk.vector(fae)
                if kt >= 4 * tc:
                    def fmask2(gp, kt=kt, t0=t0, n=smsk.n):
                        gp.wait_ge(smsk.h, n)
                        smsk.inc(gp.affine_select(
                            asb[:], asb[:], [[1, 512]], ALU.is_ge,
                            0.0, base=t0 - kt * 128, channel_multiplier=-1))
                    blk.gpsimd(fmask2)
                atok = smsk.tok()
                for c in range(2):
                    ob = obanks[c]

                    def fo(t, ob=ob, c=c, kt=kt, atok=atok, nkt=nkt):
                        wait_tok(t, atok)
                        if kt == 0:
                            ob.free.wait(t)
                        ins = t.matmul(ob.ps[:], vtl[:, kt, c * 128:(c + 1) * 128],
                                       asb[:], start=(kt == 0), stop=(kt == nkt - 1))
                        if c == 1:
                            amm.inc(ins)
                    blk.tensor(fo)
            st_b = st.n
            for c in range(2):
                ob = obanks[c]

                def foe(v, ob=ob, c=c, fin=amm.tok(), st_b=st_b):
                    wait_tok(v, fin)
                    v.wait_ge(st.h, st_b)
                    ins = v.tensor_copy(ost[:, c, :], ob.ps[:])
                    amm.inc(ins)
                    ob.free.inc(v.engine_nop())
                blk.vector(foe)
            prog.release_banks(obanks)

            def fsto(sy, t0=t0):
                amm.wait(sy)
                st.inc(sy.dma_start(
                    AP(oD, t0, [[T, 128], [128 * T, 2], [1, 512]]), ost[:]), 16)
            blk.sync(fsto)

        prog.arena_reset_to(MARK)
        och = prog.aalloc([128, 2, 512], BF16)
        gch = prog.aalloc([128, 2, 512], BF16)
        sqc = prog.aalloc([128, 2, 512], BF16)
        rstd_t = prog.aalloc([128, 512], F32)
        ofin = prog.aalloc([128, 2, 512], BF16)
        for tc in range(prog.NT):
            t0 = tc * 512
            blk = prog.phase()
            ldo = prog.sem("l0aldo")

            def fldo(sy, t0=t0):
                st.wait(sy)
                ldo.inc(sy.dma_start(och[:], AP(oD, t0, [[T, 128], [128 * T, 2], [1, 512]])), 16)
                ldo.inc(sy.dma_start(gch[:], AP(gD, t0, [[T, 128], [128 * T, 2], [1, 512]])), 16)
            blk.sync(fldo)
            sqs = prog.sem("l0asq")

            def fsq(sc):
                ldo.wait(sc)
                sc.activation(sqc[:, 0, :], och[:, 0, :], AF.Square)
                sqs.inc(sc.activation(sqc[:, 1, :], och[:, 1, :], AF.Square))
            blk.scalar(fsq)
            bms = prog.bank()
            prog.mm_acc(bms, 512, [(prog.ones_bf[:], sqc[:, c, :]) for c in range(2)],
                        waits=[sqs.tok()])
            tok = prog.evac(bms, "scalar", lambda sc, b=bms: sc.activation(
                rstd_t[:], b.ps[:], AF.Copy, scale=1.0 / DK))
            gv = prog.sem("l0agv")

            def fsqr(sc, tok=tok):
                wait_tok(sc, tok)
                gv.inc(sc.activation(rstd_t[:], rstd_t[:], AF.Sqrt, bias=1e-5))
            blk.scalar(fsqr)

            def fr(v):
                gv.wait(v)
                gv.inc(v.reciprocal(rstd_t[:], rstd_t[:]))
            blk.vector(fr)
            st_b = st.n

            def fgate(v, st_b=st_b):
                gv.wait(v)
                v.wait_ge(st.h, st_b)
                ins = None
                for c in range(2):
                    v.tensor_mul(prog.tscr[:], och[:, c, :], rstd_t[:])
                    v.tensor_scalar(prog.tscr[:], prog.tscr[:], gnc[:, c:c + 1],
                                    None, ALU.mult)
                    ins = v.tensor_mul(ofin[:, c, :], prog.tscr[:], gch[:, c, :])
                gv.inc(ins)
            blk.vector(fgate)

            def fstf(sy, t0=t0, h=h):
                gv.wait(sy)
                st.inc(sy.dma_start(
                    AP(ofD, h * 256 * T + t0, [[T, 128], [128 * T, 2], [1, 512]]),
                    ofin[:]), 16)
            blk.sync(fstf)
    prog.arena_reset(full=True)
    w_o = WeightFM(prog, W["L0a_Wo"])
    rhs = prog.aalloc([128, NCH, 512], BF16)
    blk = prog.phase()
    w_o.load(prog)
    for tc in range(prog.NT):
        t0 = tc * 512
        blk = prog.phase()

        def fldr(sy, t0=t0):
            st.wait(sy)
            ld.inc(sy.dma_start(rhs[:], dram_fm(ofD, T, t0, 512)), 16)
        blk.sync(fldr)
        st_before = st.n
        otoks = []
        for uc in range(NCH):
            b = prog.bank()
            prog.mm_acc(b, 512, [(w_o.tile(uc, dc), rhs[:, dc]) for dc in range(NCH)],
                        waits=[w_o.sem.tok(), ld.tok()])

            def fev(sc, b=b, uc=uc, st_before=st_before):
                sc.wait_ge(st.h, st_before)
                return sc.activation(prog.ybuf[:, uc], b.ps[:], AF.Copy)
            otoks.append(prog.evac(b, "scalar", fev))

        def fsty(sy, t0=t0, otoks=tuple(otoks)):
            for tk in otoks:
                wait_tok(sy, tk)
            st.inc(sy.dma_start(dram_fm(yscr, T, t0, 512), prog.ybuf[:]), 16)
        blk.sync(fsty)
    blk = prog.phase()

    def fw(sy):
        st.wait(sy)
    blk.sync(fw)
    prog.phase()


def declare_l0_attn(prog, yscr):
    nc = prog.nc
    T = prog.T
    W = {"yscr": yscr}
    W["L0a_Wq"] = declare_wfm(nc, "L0a_Wq", D, D)
    W["L0a_Wk"] = declare_wfm(nc, "L0a_Wk", D, D)
    W["L0a_Wv"] = declare_wfm(nc, "L0a_Wv", D, D)
    W["L0a_Wg"] = declare_wfm(nc, "L0a_Wg", D, D)
    W["L0a_Wo"] = declare_wfm(nc, "L0a_Wo", D, D)
    W["L0a_skw1"] = declare_wfm(nc, "L0a_skw1", D, 128)
    W["L0a_svw1"] = declare_wfm(nc, "L0a_svw1", D, 128)
    W["L0a_skw2"] = nc.dram_tensor("L0a_skw2", (128, 256), F32, kind="ExternalInput")
    W["L0a_svw2"] = nc.dram_tensor("L0a_svw2", (128, 256), F32, kind="ExternalInput")
    W["L0a_b2t"] = nc.dram_tensor("L0a_b2t", (128, 512), F32, kind="ExternalInput")
    W["L0a_gnc"] = nc.dram_tensor("L0a_gnc", (128, 2), F32, kind="ExternalInput")
    W["L0a_cos"] = nc.dram_tensor("L0a_cos", (128 * T,), F32, kind="ExternalInput")
    W["L0a_sin"] = nc.dram_tensor("L0a_sin", (128 * T,), F32, kind="ExternalInput")
    W["l0a_q"] = nc.dram_tensor("l0a_q", (256 * T,), BF16, kind="Internal")
    W["l0a_k"] = nc.dram_tensor("l0a_k", (256 * T,), BF16, kind="Internal")
    W["l0a_v"] = nc.dram_tensor("l0a_v", (T * 256,), BF16, kind="Internal")
    W["l0a_g"] = nc.dram_tensor("l0a_g", (256 * T,), BF16, kind="Internal")
    W["l0a_o"] = nc.dram_tensor("l0a_o", (256 * T,), BF16, kind="Internal")
    W["l0a_of"] = nc.dram_tensor("l0a_of", (D * T,), BF16, kind="Internal")
    return W


def prep_l0_attn(inputs, T):
    out = {}
    for nm in ("Wq", "Wk", "Wv", "Wg", "Wo"):
        out[f"L0a_{nm}"] = host_wfm(inputs[f"L0_attn_{nm}"])
    skw1 = np.asarray(inputs["L0_attn_sk_w1"])
    svw1 = np.asarray(inputs["L0_attn_sv_w1"])
    pad = np.zeros((1024, 112), np.float32)
    out["L0a_skw1"] = host_wfm(np.concatenate([skw1, pad], 1))
    out["L0a_svw1"] = host_wfm(np.concatenate([svw1, pad], 1))
    w2k = np.zeros((128, 256), np.float32)
    w2k[0:16] = np.asarray(inputs["L0_attn_sk_w2"])
    out["L0a_skw2"] = w2k
    w2v = np.zeros((128, 256), np.float32)
    w2v[0:16] = np.asarray(inputs["L0_attn_sv_w2"])
    out["L0a_svw2"] = w2v
    b2 = np.zeros((128, 512), np.float32)
    b2[:, 0:256] = np.broadcast_to(np.asarray(inputs["L0_attn_sk_b2"])[None, :], (128, 256))
    b2[:, 256:512] = np.broadcast_to(np.asarray(inputs["L0_attn_sv_b2"])[None, :], (128, 256))
    out["L0a_b2t"] = b2
    gn = np.asarray(inputs["L0_attn_gnorm_w"]).reshape(2, 128)
    out["L0a_gnc"] = np.ascontiguousarray(gn.T).astype(np.float32)
    half = 128
    inv_freq = 1.0 / 10000.0 ** (np.arange(half, dtype=np.float64) / half)
    ang = np.arange(T, dtype=np.float64)[:, None] * inv_freq[None, :]
    out["L0a_cos"] = np.ascontiguousarray(np.cos(ang).T.astype(np.float32)).reshape(-1)
    out["L0a_sin"] = np.ascontiguousarray(np.sin(ang).T.astype(np.float32)).reshape(-1)
    return out



DBGH = 0
DBGI = 0


def emit_l2_attn(prog, W, debug_stop=None):
    nc = prog.nc
    T = prog.T
    yscr = W["yscr"]
    NH, HD, K, WIN = 8, 128, 32, 64
    ld = prog.sem("l2ald")
    st = prog.sem("l2ast")
    qeD, vtD, gramD, uD, wdD, osD = (W["l2a_qe"], W["l2a_vt"], W["l2a_gram"],
                                     W["l2a_u"], W["l2a_wd"], W["l2a_os"])

    prog.arena_reset(full=True)
    sigc = prog.aalloc([128, 8], F32)
    metc = prog.aalloc([128, 8], F32)
    hmsb = prog.aalloc([128, NH, 128], BF16)
    metc_bf = prog.aalloc([128, 8], BF16)
    mask01 = prog.aalloc([128, 64], F32)
    maskm1 = prog.aalloc([128, 64], F32)
    maskim = prog.aalloc([128, 64], F32)
    zt64 = prog.aalloc([128, 64], BF16)
    onesc = prog.aalloc([128, 1], BF16)
    MARK = prog.arena_mark()
    w_e = WeightFM(prog, dram=None, dtype=BF16, n_uc=1, n_dc=8)
    qtile = prog.aalloc([128, T], BF16)
    sred = prog.aalloc([128, 4], F32)
    blk = prog.phase()
    msk = prog.sem("l2amsk")

    def fmsk(gp):
        gp.memset(mask01[:], 1.0)
        gp.memset(maskm1[:], 0.0)
        gp.memset(maskim[:], 0.0)
        gp.memset(zt64[:], 0.0)
        gp.memset(onesc[:], 1.0)
        gp.affine_select(mask01[:], mask01[:], [[1, 64]], ALU.is_ge, 0.0,
                         base=-63, channel_multiplier=1)
        msk.inc(gp.memset(sigc[:], 0.0))
    blk.gpsimd(fmsk)

    def fmsk2(v):
        msk.wait(v)
        ins = v.tensor_scalar(maskm1[:], mask01[:], -1.0, None, ALU.mult)
        msk.inc(ins)
        v.wait_ge(msk.h, msk.n)
        ins = v.tensor_scalar(maskim[:], mask01[:], 1.0, None, ALU.subtract)
        msk.inc(ins)
    blk.vector(fmsk2)
    for h in range(NH):
        blk = prog.phase()

        def fldw(gp, h=h):
            k = 8 * 128
            w_e.sem.inc(gp.dma_start(w_e.sb[:, 0:k], W["L2a_encq"][h]), 16)
        blk.gpsimd(fldw)
        qsig = prog.sem("l2aqs")
        for tc in range(prog.NT):
            t0 = tc * 512
            b = prog.bank()
            prog.mm_acc(b, 512, [(w_e.tile(0, dc), prog.xn[:, dc, t0:t0 + 512])
                                 for dc in range(NCH)], waits=[w_e.sem.tok()])
            prog.evac(b, "scalar", lambda sc, b=b, t0=t0: sc.activation(
                qtile[:, t0:t0 + 512], b.ps[:], AF.Relu))
        blk = prog.phase()

        def fsig(v, h=h):
            def step(ins):
                qsig.inc(ins)
                v.wait_ge(qsig.h, qsig.n)
            step(v.reduce_sum(sred[:, 0:1], qtile[:], AX.X))
            step(v.reduce_sum(sred[:, 1:2],
                              AP(qtile.tensor, qtile.offset, [[qtile.ap[0][0], 128],
                                                              [8, T // 8]]), AX.X))
            step(v.tensor_scalar(sred[:, 0:1], sred[:, 0:1], 1.0 / T, None, ALU.mult))
            step(v.tensor_scalar(sred[:, 1:2], sred[:, 1:2], 0.5 * 8.0 / T, None, ALU.mult))
            step(v.tensor_add(sigc[:, h:h + 1], sred[:, 0:1], sred[:, 1:2]))
        blk.vector(fsig)

        def fstq(sy, h=h):
            qsig.wait(sy)
            st.inc(sy.dma_start(AP(qeD, h * 128 * T, [[T, 128], [1, T]]), qtile[:]), 16)
        blk.sync(fstq)
        blk = prog.phase()

        def fldv(gp, h=h):
            k = 8 * 128
            w_e.sem.inc(gp.dma_start(w_e.sb[:, 0:k], W["L2a_encv"][h]), 16)
        blk.gpsimd(fldv)
        for kt in range(prog.NTT):
            k0 = kt * 128
            b = prog.bank()
            st_b = st.n
            prog.mm_acc(b, 128, [(prog.xn[:, dc, k0:k0 + 128], w_e.tile(0, dc))
                                 for dc in range(NCH)], waits=[w_e.sem.tok()])

            def fve(sc, b=b, st_b=st_b):
                sc.wait_ge(st.h, st_b)
                return sc.activation(qtile[:, 0:128], b.ps[:, 0:128], AF.Relu)
            tkv = prog.evac(b, "scalar", fve)

            def fstv(sy, h=h, kt=kt, tkv=tkv):
                wait_tok(sy, tkv)
                st.inc(sy.dma_start(
                    AP(vtD, (kt * NH + h) * 128 * 128, [[128, 128], [1, 128]]),
                    qtile[:, 0:128]), 16)
            blk.sync(fstv)
    if debug_stop == "l2a_q":
        emit_copy_scr(prog, qeD, yscr, dt=BF16)
        return
    blk = prog.phase()
    w_m1 = WeightFM(prog, W["L2a_mnw1"])
    w_m2 = WeightFM(prog, W["L2a_mnw2"])
    mncols = prog.aalloc([128, 16], F32)
    hidc = prog.aalloc([128, 4], BF16)
    sigc_bf = prog.aalloc([128, 8], BF16)
    mscr = prog.aalloc([128, 8], F32)
    w_m1.load(prog)
    w_m2.load(prog)

    def fldmn(gp):
        ld.inc(gp.dma_start(mncols[:], W["L2a_mncols"][:]), 16)
    blk.gpsimd(fldmn)
    mn = prog.sem("l2amn")

    def fsigbf(v):
        wait_tok(v, prog.sem("l2aqs").tok())
        mn.inc(v.tensor_copy(sigc_bf[:], sigc[:]))
    blk.vector(fsigbf)
    sbf_tok = mn.tok()
    for uc in range(4):
        b = prog.bank()
        prog.mm_acc(b, 1, [(w_m1.tile(uc, dc), sigc_bf[:, dc:dc + 1]) for dc in range(NCH)],
                    waits=[w_m1.sem.tok(), ld.tok(), sbf_tok])
        prog.evac(b, "scalar", lambda sc, b=b, uc=uc: sc.activation(
            hidc[:, uc:uc + 1], b.ps[:, 0:1], AF.Gelu, bias=mncols[:, uc:uc + 1]))
    blk = prog.phase()
    for uc in range(NCH):
        b = prog.bank()
        prog.mm_acc(b, 1, [(w_m2.tile(uc, dc), hidc[:, dc:dc + 1]) for dc in range(4)],
                    waits=[w_m2.sem.tok()])

        def fmet2(v, b=b, uc=uc):
            def step(ins):
                mn.inc(ins)
                v.wait_ge(mn.h, mn.n)
            step(v.tensor_scalar(mscr[:, 0:1], b.ps[:, 0:1], 0.1, None, ALU.mult))
            step(v.tensor_add(metc[:, uc:uc + 1], mscr[:, 0:1], mncols[:, 4 + uc:5 + uc]))
        prog.evac(b, "vector", lambda v, b=b, uc=uc: (fmet2(v), v.engine_nop())[-1])
    blk = prog.phase()
    sp = prog.sem("l2asp")

    def fsp1(sc):
        mn.wait(sc)
        sp.inc(sc.activation(mscr[:, 0:8], metc[:, 0:8], AF.Abs))
    blk.scalar(fsp1)

    def fsp2(sc):
        sp.wait(sc)
        sp.inc(sc.activation(mscr[:, 0:8], mscr[:, 0:8], AF.Exp, scale=-1.0))
    blk.scalar(fsp2)

    def fsp3(sc):
        sp.wait(sc)
        sp.inc(sc.activation(mscr[:, 0:8], mscr[:, 0:8], AF.Ln, bias=1.0))
    blk.scalar(fsp3)

    def fsp4(v):
        sp.wait(v)
        ins = v.tensor_scalar_max(metc[:, 0:8], metc[:, 0:8], 0.0)
        sp.inc(ins)
        v.wait_ge(sp.h, sp.n)
        ins = v.tensor_add(metc[:, 0:8], metc[:, 0:8], mscr[:, 0:8])
        sp.inc(ins)
        v.wait_ge(sp.h, sp.n)
        sp.inc(v.tensor_copy(metc_bf[:], metc[:, 0:8]))
    blk.vector(fsp4)
    blk = prog.phase()
    uvsb = prog.aalloc([128, 2, 128], BF16)
    huv = prog.sem("l2ahuv")
    for h in range(NH):
        def flduv(gp, h=h):
            huv.inc(gp.dma_start(uvsb[0:32, :, :],
                                 AP(W["L2a_hUV"], h * 2 * 32 * 128,
                                    [[128, 32], [32 * 128, 2], [1, 128]])), 16)
        blk.gpsimd(flduv)
        b = prog.bank()

        def fhm(t, b=b, n=huv.n):
            t.wait_ge(huv.h, n)
            b.free.wait(t)
            b.mm.inc(t.matmul(b.ps[:, 0:128], uvsb[0:32, 0, :], uvsb[0:32, 1, :],
                              start=True, stop=True))
        blk.tensor(fhm)
        prog.evac(b, "vector", lambda v, b=b, h=h: v.tensor_copy(
            hmsb[:, h, :], b.ps[:, 0:128]))

    prog.arena_reset_to(MARK)
    qh = prog.aalloc([128, T], BF16)
    mqh = prog.aalloc([128, T], BF16)
    sqh = prog.aalloc([128, T], BF16)
    vth = prog.aalloc([128, prog.NTT, 128], BF16)
    gsb = prog.aalloc([128, 256], F32)
    band = prog.aalloc([128, 64], F32)
    usl = prog.aalloc([128, 64], F32)
    d2 = prog.aalloc([128, 64], F32)
    nd0 = prog.aalloc([128, 64], F32)
    nd1 = prog.aalloc([128, 64], F32)
    wsel = prog.aalloc([128, 64], BF16)
    m8 = prog.aalloc([128, 8], F32)
    ucol = prog.aalloc([128, 8], F32)
    ucolr = prog.aalloc([128, 2], F32)
    wtsb = prog.aalloc([128, 256], BF16)
    ctxT = prog.aalloc([128, 128], BF16)
    rowb = prog.aalloc([128, 2, 128], F32)
    osb = prog.aalloc([128, 128], BF16)
    blk = prog.phase()
    zpad = prog.sem("l2azp")

    def fzpad2(gp):
        gp.memset(d2[:], 0.0)
        zpad.inc(gp.memset(wsel[:], 0.0))
    blk.gpsimd(fzpad2)

    def fzpad3(sy):
        zpad.wait(sy)
        zpad.inc(sy.dma_start(AP(uD, 0, [[64, 2], [1, 64]]), d2[0:2, 0:64]), 16)
        for j in range(4):
            zpad.inc(sy.dma_start(AP(wdD, j * 64, [[256, 128], [1, 64]]), wsel[:]), 16)
        zpad.wait(sy)
    blk.sync(fzpad3)
    for h in range(NH):
        blk = prog.phase()
        lda = prog.sem("l2alda")

        def fldh(sy, h=h):
            st.wait(sy)
            lda.inc(sy.dma_start(qh[:], AP(qeD, h * 128 * T, [[T, 128], [1, T]])), 16)
            lda.inc(sy.dma_start(vth[:], AP(vtD, h * 128 * 128,
                                            [[128, 128], [NH * 128 * 128, prog.NTT],
                                             [1, 128]])), 16)
        blk.sync(fldh)
        pre = prog.sem("l2apre")

        def fmq(v, h=h):
            lda.wait(v)
            ins = v.tensor_scalar(mqh[:], qh[:], metc[:, h:h + 1], None, ALU.mult)
            pre.inc(ins)
        blk.vector(fmq)

        def fsq(sc):
            lda.wait(sc)
            pre.inc(sc.activation(sqh[:], qh[:], AF.Square))
        blk.scalar(fsq)
        for i in range(prog.NTT):
            b = prog.bank()

            def fu(t, b=b, i=i, h=h, n=pre.n):
                t.wait_ge(pre.h, n)
                b.free.wait(t)
                t.matmul(b.ps[:, 0:1], sqh[:, i * 128:(i + 1) * 128],
                         metc_bf[:, h:h + 1], start=True, stop=True)
                b.mm.inc(t.matmul(b.ps[:, 1:2], sqh[:, i * 128:(i + 1) * 128],
                                  onesc[:], start=True, stop=True,
                                  skip_group_check=True))
            blk.tensor(fu)
            st_bu = st.n
            tke = prog.evac(b, "vector", lambda v, b=b, st_bu=st_bu: (
                v.wait_ge(st.h, st_bu),
                v.tensor_copy(ucol[:, 0:2], b.ps[:, 0:2]))[-1])

            def fstu(sy, i=i, tke=tke):
                wait_tok(sy, tke)
                st.inc(sy.dma_start(AP(uD, 128 + i * 128, [[1, 128], [1, 1]]),
                                    ucol[:, 0:1]), 16)
                st.inc(sy.dma_start(AP(uD, 128 + T + 128 + i * 128, [[1, 128], [1, 1]]),
                                    ucol[:, 1:2]), 16)
            blk.sync(fstu)
        for i in range(prog.NTT):
            blk = prog.phase()
            t0 = i * 128
            bg = prog.bank()

            def fg(t, bg=bg, i=i, t0=t0):
                lda.wait(t)
                pre.wait(t)
                bg.free.wait(t)
                ins = None
                if i > 0:
                    ins = t.matmul(bg.ps[:, 0:128], qh[:, t0:t0 + 128],
                                   mqh[:, (i - 1) * 128:i * 128], start=True, stop=True)
                ins2 = t.matmul(bg.ps[:, 128:256], qh[:, t0:t0 + 128],
                                mqh[:, t0:t0 + 128], start=(i == 0), stop=True,
                                skip_group_check=True)
                bg.mm.inc(ins2)
            blk.tensor(fg)
            gd = prog.sem("l2agd")
            dv = prog.sem("l2adv")

            def fge(v, bg=bg, i=i, n=gd.n):
                v.wait_ge(gd.h, n)
                if i == 0:
                    ins = v.tensor_copy(gsb[:, 128:256], bg.ps[:, 128:256])
                    gd.inc(ins)
                    v.wait_ge(gd.h, gd.n)
                    return v.memset(gsb[:, 0:128], 0.0)
                return v.tensor_copy(gsb[:], bg.ps[:, 0:256])
            gtk = prog.evac(bg, "vector", fge)

            def fgout(sy, i=i, gtk=gtk):
                wait_tok(sy, gtk)
                gd.inc(sy.dma_start(AP(gramD, 0, [[256, 128], [1, 256]]), gsb[:]), 16)
            blk.sync(fgout)

            def fgin(sy, i=i, ndv=dv.n):
                gd.wait(sy)
                sy.wait_ge(dv.h, ndv)
                gd.inc(sy.dma_start(band[:], AP(gramD, 65, [[257, 128], [1, 64]])), 16)
                gd.inc(sy.dma_start(usl[:], AP(uD, 128 + (i - 1) * 128 + 65,
                                               [[1, 128], [1, 64]])), 16)
                gd.inc(sy.dma_start(ucolr[:, 0:1],
                                    AP(uD, 128 + i * 128, [[1, 128], [1, 1]])), 16)
                if i == 0:
                    gd.inc(sy.dma_start(rowb[0:1, 1, 0:1],
                                        AP(uD, 128, [[1, 1], [1, 1]])), 16)
            blk.sync(fgin)
            def fdist(v, i=i):
                gd.wait(v)

                def step(ins):
                    dv.inc(ins)
                    v.wait_ge(dv.h, dv.n)
                step(v.scalar_tensor_tensor(d2[:], band[:], -2.0, usl[:],
                                            ALU.mult, ALU.add))
                step(v.tensor_scalar(d2[:], d2[:], ucolr[:, 0:1], 1e-8,
                                     ALU.add, ALU.add))
                step(v.tensor_scalar_max(d2[:], d2[:], 1e-8))
            blk.vector(fdist)

            def fsqrt(sc):
                dv.wait(sc)
                dv.inc(sc.activation(d2[:], d2[:], AF.Sqrt))
            blk.scalar(fsqrt)

            if i == 0:
                bu0 = prog.bank()

                def fb0(t, bu0=bu0, n=gd.n):
                    t.wait_ge(gd.h, n)
                    bu0.free.wait(t)
                    bu0.mm.inc(t.matmul(bu0.ps[:, 0:1], prog.ones_f1[:],
                                        rowb[0:1, 1, 0:1], start=True, stop=True))
                blk.tensor(fb0)

                def fd0(v, b=bu0):
                    b.mm.wait(v)

                    def step(ins):
                        dv.inc(ins)
                        v.wait_ge(dv.h, dv.n)
                    step(v.scalar_tensor_tensor(m8[:, 5:6], gsb[:, 128:129], -2.0,
                                                ucolr[:, 0:1], ALU.mult, ALU.add))
                    step(v.tensor_add(m8[:, 5:6], m8[:, 5:6], b.ps[:, 0:1]))
                    step(v.tensor_scalar(m8[:, 5:6], m8[:, 5:6], 1e-8, None, ALU.add))
                    step(v.tensor_scalar_max(m8[:, 5:6], m8[:, 5:6], 1e-8))
                    b.free.inc(v.engine_nop())
                blk.vector(fd0)

                def fd0s(sc):
                    dv.wait(sc)
                    dv.inc(sc.activation(m8[:, 5:6], m8[:, 5:6], AF.Sqrt))
                blk.scalar(fd0s)

            def fw0(v, i=i, n=dv.n):
                v.wait_ge(dv.h, n)

                def step(ins):
                    dv.inc(ins)
                    v.wait_ge(dv.h, dv.n)
                if i == 0:
                    step(v.tensor_mul(nd0[:], d2[:], mask01[:]))
                    step(v.tensor_scalar(nd1[:], maskim[:], m8[:, 5:6], None, ALU.mult))
                    step(v.tensor_sub(d2[:], nd0[:], nd1[:]))
                step(v.tensor_scalar(nd0[:], d2[:], -1.0, None, ALU.mult))
            blk.vector(fw0)
            tsel = prog.sem("l2ats")

            def ftop(v):
                dv.wait(v)
                src, dst = nd0, nd1
                for r in range(4):
                    ins = v.max(out=m8[:], in_=src[:])
                    tsel.inc(ins)
                    v.wait_ge(tsel.h, tsel.n)
                    ins = v.match_replace(out=dst[:], in_to_replace=m8[:],
                                          in_values=src[:], imm_value=-1e30)
                    tsel.inc(ins)
                    v.wait_ge(tsel.h, tsel.n)
                    src, dst = dst, src
                ins = v.tensor_scalar(d2[:], d2[:], -1.0, None, ALU.mult)
                tsel.inc(ins)
                v.wait_ge(tsel.h, tsel.n)
                ins = v.tensor_scalar(nd1[:], src[:], -1e29, None, ALU.is_le)
                tsel.inc(ins)
            blk.vector(ftop)

            def fexp(sc):
                tsel.wait(sc)
                tsel.inc(sc.activation(d2[:], d2[:], AF.Exp))
            blk.scalar(fexp)

            wd = prog.sem("l2awd")

            def fwn(v, i=i):
                tsel.wait(v)
                v.wait_ge(wd.h, wd.n)

                def step(ins):
                    tsel.inc(ins)
                    v.wait_ge(tsel.h, tsel.n)
                step(v.tensor_mul(d2[:], d2[:], nd1[:]))
                step(v.reduce_sum(m8[:, 0:1], d2[:], AX.X))
                step(v.tensor_scalar(m8[:, 1:2], m8[:, 0:1], 1e-8, None, ALU.add))
                step(v.reciprocal(m8[:, 2:3], m8[:, 1:2]))
                step(v.tensor_scalar(d2[:], d2[:], m8[:, 2:3], None, ALU.mult))
                if i == 0:
                    step(v.tensor_mul(nd0[:], d2[:], maskim[:]))
                    step(v.reduce_sum(m8[:, 3:4], nd0[:], AX.X))
                    step(v.tensor_scalar(m8[:, 3:4], m8[:, 3:4], -1.0, None, ALU.mult))
                    step(v.tensor_mul(d2[:], d2[:], mask01[:]))
                step(v.tensor_copy(wsel[:], d2[:]))
            blk.vector(fwn)
            if debug_stop == "l2a_tile" and h == DBGH and i == DBGI:
                dbgt = prog.sem("l2adbg")

                def fdt(sy):
                    tsel.wait(sy)
                    dv.wait(sy)
                    dbgt.inc(sy.dma_start(AP(yscr, 0, [[64, 128], [1, 64]]), d2[:]), 16)
                    dbgt.inc(sy.dma_start(AP(yscr, 128 * 64, [[64, 128], [1, 64]]), band[:]), 16)
                    dbgt.inc(sy.dma_start(AP(yscr, 2 * 128 * 64, [[64, 128], [1, 64]]), usl[:]), 16)
                    dbgt.inc(sy.dma_start(AP(yscr, 3 * 128 * 64, [[1, 128], [1, 1]]), ucolr[:, 0:1]), 16)
                    dbgt.inc(sy.dma_start(AP(yscr, 3 * 128 * 64 + 256, [[8, 128], [1, 8]]), m8[:]), 16)
                    dbgt.inc(sy.dma_start(AP(yscr, 3 * 128 * 64 + 2048, [[64, 128], [1, 64]]), nd1[:]), 16)
                    dbgt.wait(sy)
                blk.sync(fdt)

            def fwout(sy, i=i):
                tsel.wait(sy)
                wd.inc(sy.dma_start(AP(wdD, 65, [[257, 128], [1, 64]]), wsel[:]), 16)
            blk.sync(fwout)

            cxd = prog.sem("l2acxd")

            def fwin(sy, ncx=None):
                wd.wait(sy)
                sy.wait_ge(cxd.h, cxd.n)
                wd.inc(sy.dma_start(wtsb[:], AP(wdD, 0, [[256, 128], [1, 256]])), 16)
            blk.sync(fwin)
            wfix = prog.sem("l2awf")

            def ffixw(v, i=i):
                wd.wait(v)
                if i == 0:
                    ins = v.tensor_scalar(wtsb[:, 128:129], zt64[:, 0:1], m8[:, 3:4],
                                          None, ALU.add)
                    wfix.inc(ins)
                    v.wait_ge(wfix.h, wfix.n)
                wfix.inc(v.engine_nop())
            blk.vector(ffixw)
            btp = prog.bank()
            btc = prog.bank()

            def ftw(t, btp=btp, btc=btc, i=i, n=wfix.n):
                t.wait_ge(wfix.h, n)
                btp.free.wait(t)
                btc.free.wait(t)
                if i > 0:
                    btp.mm.inc(t.transpose(btp.ps[:].bitcast(BF16)[:, 0:128],
                                           wtsb[:, 0:128], prog.ident[:]))
                btc.mm.inc(t.transpose(btc.ps[:].bitcast(BF16)[:, 0:128],
                                       wtsb[:, 128:256], prog.ident[:]))
            blk.tensor(ftw)
            twk = prog.sem("l2atw")

            def ftwe(v, btp=btp, btc=btc, i=i):
                btc.mm.wait(v)
                if i > 0:
                    btp.mm.wait(v)
                    v.tensor_copy(wtsb[:, 0:128], btp.ps[:].bitcast(BF16)[:, 0:128])
                ins = v.tensor_copy(wtsb[:, 128:256], btc.ps[:].bitcast(BF16)[:, 0:128])
                twk.inc(ins)
                btp.free.inc(v.engine_nop())
                btc.free.inc(v.engine_nop())
            blk.vector(ftwe)
            bcx = prog.bank()

            def fcx(t, bcx=bcx, i=i, n=twk.n):
                t.wait_ge(twk.h, n)
                bcx.free.wait(t)
                ins = None
                if i > 0:
                    ins = t.matmul(bcx.ps[:, 0:128], vth[:, i - 1, :], wtsb[:, 0:128],
                                   start=True, stop=True)
                ins = t.matmul(bcx.ps[:, 0:128], vth[:, i, :], wtsb[:, 128:256],
                               start=(i == 0), stop=True, skip_group_check=True)
                bcx.mm.inc(ins)
            blk.tensor(fcx)

            def fcxd(t):
                prog.sem("l2acxd").inc(t.nop())
            blk.tensor(fcxd)
            ctk = prog.evac(bcx, "vector", lambda v, b=bcx: v.tensor_copy(
                ctxT[:], b.ps[:, 0:128]))
            qnr = prog.sem("l2aqnr")

            def fldqn(sy, i=i):
                st.wait(sy)
                qnr.inc(sy.dma_start(rowb[0:1, 0, :],
                                     AP(uD, 128 + T + 128 + i * 128, [[1, 1], [1, 128]])), 16)
            blk.sync(fldqn)
            hrow = prog.sem("l2ahr")

            def fqn2(v):
                qnr.wait(v)

                def step(ins):
                    hrow.inc(ins)
                    v.wait_ge(hrow.h, hrow.n)
                step(v.tensor_scalar(rowb[0:1, 0, :], rowb[0:1, 0, :], 1e-24, None,
                                     ALU.max))
            blk.vector(fqn2)

            def fqn3(sc):
                hrow.wait(sc)
                hrow.inc(sc.activation(rowb[0:1, 0, :], rowb[0:1, 0, :], AF.Sqrt))
            blk.scalar(fqn3)

            def fqn4(v):
                hrow.wait(v)

                def step(ins):
                    hrow.inc(ins)
                    v.wait_ge(hrow.h, hrow.n)
                step(v.tensor_scalar(rowb[0:1, 1, :], rowb[0:1, 0, :], 0.2, None,
                                     ALU.is_gt))
                step(v.tensor_scalar(rowb[0:1, 0, :], rowb[0:1, 0, :], 1e-12, None,
                                     ALU.max))
                step(v.reciprocal(rowb[0:1, 0, :], rowb[0:1, 0, :]))
                step(v.tensor_mul(rowb[0:1, 0, :], rowb[0:1, 0, :], rowb[0:1, 1, :]))
                step(v.tensor_scalar(rowb[0:1, 0, :], rowb[0:1, 0, :], 0.1, None,
                                     ALU.mult))
            blk.vector(fqn4)
            bfb = prog.bank()

            def ffb(t, bfb=bfb, n=hrow.n):
                t.wait_ge(hrow.h, n)
                bfb.free.wait(t)
                bfb.mm.inc(t.matmul(bfb.ps[:, 0:128], prog.ones_f1[:],
                                    rowb[0:1, 0, :], start=True, stop=True))
            blk.tensor(ffb)
            fbk = prog.evac(bfb, "vector", lambda v, b=bfb: v.tensor_copy(
                rowb[:, 1, :], b.ps[:, 0:128]))
            bhb = prog.bank()

            def fhb(t, bhb=bhb, h=h, i=i, t0=t0, n=pre.n):
                t.wait_ge(pre.h, n)
                bhb.free.wait(t)
                bhb.mm.inc(t.matmul(bhb.ps[:, 0:128], hmsb[:, h, :],
                                    qh[:, t0:t0 + 128], start=True, stop=True))
            blk.tensor(fhb)
            ho = prog.sem("l2aho")
            st_b2 = st.n

            def fout(v, bhb=bhb, i=i, t0=t0, ctk=ctk, fbk=fbk, st_b2=st_b2):
                bhb.mm.wait(v)
                wait_tok(v, ctk)
                wait_tok(v, fbk)
                v.wait_ge(st.h, st_b2)

                def step(ins):
                    ho.inc(ins)
                    v.wait_ge(ho.h, ho.n)
                step(v.tensor_mul(rowb[:, 0, :], bhb.ps[:, 0:128], rowb[:, 1, :]))
                step(v.tensor_add(rowb[:, 0, :], rowb[:, 0, :], ctxT[:]))
                step(v.tensor_mul(osb[:], qh[:, t0:t0 + 128], rowb[:, 0, :]))
                bhb.free.inc(v.engine_nop())
            blk.vector(fout)

            def fsto(sy, h=h, t0=t0):
                ho.wait(sy)
                st.inc(sy.dma_start(
                    AP(osD, h * 128 * T + t0, [[T, 128], [1, 128]]), osb[:]), 16)
            blk.sync(fsto)
            if debug_stop == "l2a_tile" and h == DBGH and i == DBGI:
                dbg2 = prog.sem("l2adb2")

                def fdt2(gp):
                    ho.wait(gp)
                    dbg2.inc(gp.dma_start(AP(yscr, 32768, [[128, 128], [1, 128]]), ctxT[:]), 16)
                    dbg2.inc(gp.dma_start(AP(yscr, 49152, [[128, 128], [1, 128]]),
                                          rowb[:, 1, :]), 16)
                    dbg2.wait(gp)
                blk.gpsimd(fdt2)
    if debug_stop == "l2a_met":
        blk = prog.phase()
        dbg = prog.sem("dbg")

        def fdm(sy):
            sp.wait(sy)
            dbg.inc(sy.dma_start(AP(yscr, 0, [[16, 128], [1, 8]]), metc[:, 0:8]), 16)
            dbg.inc(sy.dma_start(AP(yscr, 4096, [[16, 128], [1, 8]]), sigc[:, 0:8]), 16)
            dbg.wait(sy)
        blk.sync(fdm)
        return
    if debug_stop == "l2a_tile":
        return
    if debug_stop == "l2a_os":
        emit_copy_scr(prog, osD, yscr, dt=BF16)
        return
    prog.arena_reset(full=True)
    w_d = WeightFM(prog, W["L2a_dec"])
    rhs = prog.aalloc([128, NCH, 512], BF16)
    blk = prog.phase()
    w_d.load(prog)
    for tc in range(prog.NT):
        t0 = tc * 512
        blk = prog.phase()

        def fldr(sy, t0=t0):
            st.wait(sy)
            ld.inc(sy.dma_start(rhs[:], dram_fm(osD, T, t0, 512)), 16)
        blk.sync(fldr)
        st_before = st.n
        otoks = []
        for uc in range(NCH):
            b = prog.bank()
            prog.mm_acc(b, 512, [(w_d.tile(uc, dc), rhs[:, dc]) for dc in range(NCH)],
                        waits=[w_d.sem.tok(), ld.tok()])

            def fev(sc, b=b, uc=uc, st_before=st_before):
                sc.wait_ge(st.h, st_before)
                return sc.activation(prog.ybuf[:, uc], b.ps[:], AF.Copy)
            otoks.append(prog.evac(b, "scalar", fev))

        def fsty(sy, t0=t0, otoks=tuple(otoks)):
            for tk in otoks:
                wait_tok(sy, tk)
            st.inc(sy.dma_start(dram_fm(yscr, T, t0, 512), prog.ybuf[:]), 16)
        blk.sync(fsty)
    blk = prog.phase()

    def fw(sy):
        st.wait(sy)
    blk.sync(fw)
    prog.phase()


def declare_l2_attn(prog, yscr):
    nc = prog.nc
    T = prog.T
    W = {"yscr": yscr}
    W["L2a_encq"] = nc.dram_tensor("L2a_encq", (8, 128, 8, 128), F32, kind="ExternalInput")
    W["L2a_encv"] = nc.dram_tensor("L2a_encv", (8, 128, 8, 128), F32, kind="ExternalInput")
    W["L2a_mnw1"] = declare_wfm(nc, "L2a_mnw1", D, 512)
    W["L2a_mnw2"] = declare_wfm(nc, "L2a_mnw2", 512, D)
    W["L2a_mncols"] = nc.dram_tensor("L2a_mncols", (128, 16), F32, kind="ExternalInput")
    W["L2a_hUV"] = nc.dram_tensor("L2a_hUV", (8 * 2 * 32 * 128,), F32, kind="ExternalInput")
    W["L2a_dec"] = declare_wfm(nc, "L2a_dec", D, D)
    W["l2a_qe"] = nc.dram_tensor("l2a_qe", (D * T,), BF16, kind="Internal")
    W["l2a_vt"] = nc.dram_tensor("l2a_vt", (T * 8 * 128,), BF16, kind="Internal")
    W["l2a_gram"] = nc.dram_tensor("l2a_gram", (128 * 256,), F32, kind="Internal")
    W["l2a_u"] = nc.dram_tensor("l2a_u", (2 * T + 512,), F32, kind="Internal")
    W["l2a_wd"] = nc.dram_tensor("l2a_wd", (128 * 256,), BF16, kind="Internal")
    W["l2a_os"] = nc.dram_tensor("l2a_os", (D * T,), BF16, kind="Internal")
    return W


def prep_l2_attn(inputs):
    out = {}
    eq = np.asarray(inputs["L2_attn_encoder_query"])
    ev = np.asarray(inputs["L2_attn_encoder_value"])
    out["L2a_encq"] = np.stack([host_wfm(eq[h]) for h in range(8)], 0)
    out["L2a_encv"] = np.stack([host_wfm(ev[h]) for h in range(8)], 0)
    out["L2a_mnw1"] = host_wfm(np.asarray(inputs["L2_attn_mn_w1"]))
    out["L2a_mnw2"] = host_wfm(np.asarray(inputs["L2_attn_mn_w2"]))
    mc = np.zeros((128, 16), np.float32)
    mc[:, 0:4] = np.asarray(inputs["L2_attn_mn_b1"]).reshape(4, 128).T
    b2 = np.asarray(inputs["L2_attn_mn_b2"]).reshape(8, 128)
    bm = np.asarray(inputs["L2_attn_base_metric"])
    for h in range(8):
        mc[:, 4 + h] = bm[h] + 0.0
    for h in range(8):
        mc[:, 4 + h] = bm[h] + 0.1 * b2[h]
    out["L2a_mncols"] = mc
    U = np.asarray(inputs["L2_attn_hebbian_U"])
    V = np.asarray(inputs["L2_attn_hebbian_V"])
    uv = np.zeros((8, 2, 32, 128), np.float32)
    for h in range(8):
        uv[h, 0] = U[h].T
        uv[h, 1] = V[h]
    out["L2a_hUV"] = uv.reshape(-1)
    out["L2a_dec"] = host_wfm(np.asarray(inputs["L2_attn_decoder"]))
    return out



def emit_l1_attn(prog, W, debug_stop=None):
    nc = prog.nc
    T = prog.T
    yscr = W["yscr"]
    NH, P = 16, 64
    scale = 1.0 / math.sqrt(P)
    ld = prog.sem("l1ald")
    st = prog.sem("l1ast")
    ctxdram = W["l1a_ctxdram"]

    prog.arena_reset(full=True)
    w_q = WeightFM(prog, W["L1a_qw"])
    w_k = WeightFM(prog, W["L1a_kw"])
    w_v = WeightFM(prog, W["L1a_vw"])
    qTc = prog.aalloc([128, T], BF16)
    kTc = prog.aalloc([128, T], BF16)
    vhb = prog.aalloc([128, prog.NTT, P], BF16)
    estrip = prog.aalloc([128, T], BF16)
    ctxacc = prog.aalloc([128, 4, 512], F32)
    vsc = prog.aalloc([128, 2, P], BF16)
    cpiece = prog.aalloc([128, 512], BF16)
    zp = prog.aalloc([128, 8], F32)
    rz = prog.aalloc([128, 2], F32)
    bqk = prog.aalloc([128, 16], F32)
    vbt = prog.aalloc([128, NH * P], F32)
    blk = prog.phase()
    w_q.load(prog)
    w_k.load(prog)
    w_v.load(prog)

    def fldb(gp):
        ld.inc(gp.dma_start(bqk[:], W["L1a_bqk"][:]), 16)
        ld.inc(gp.dma_start(vbt[:], W["L1a_vbt"][:]), 16)
    blk.gpsimd(fldb)

    ctmm = prog.sem("l1actm")
    czp = prog.sem("l1azp")
    for h in range(NH):
        uc = h // 2
        sub = (h % 2) * 64
        if h % 2 == 0:
            blk = prog.phase()
            for tc in range(prog.NT):
                t0 = tc * 512
                bq = prog.bank()
                prog.mm_acc(bq, 512, [(w_q.tile(uc, dc), prog.xn[:, dc, t0:t0 + 512])
                                      for dc in range(NCH)],
                            waits=[w_q.sem.tok(), ld.tok()])
                prog.evac(bq, "scalar", lambda sc, b=bq, t0=t0, uc=uc: sc.activation(
                    qTc[:, t0:t0 + 512], b.ps[:], AF.Identity, bias=bqk[:, uc:uc + 1]))
                bk = prog.bank()
                prog.mm_acc(bk, 512, [(w_k.tile(uc, dc), prog.xn[:, dc, t0:t0 + 512])
                                      for dc in range(NCH)],
                            waits=[w_k.sem.tok(), ld.tok()])
                prog.evac(bk, "scalar", lambda sc, b=bk, t0=t0, uc=uc: sc.activation(
                    kTc[:, t0:t0 + 512], b.ps[:], AF.Identity, bias=bqk[:, 8 + uc:9 + uc]))
        blk = prog.phase()
        vtoks = []
        for kt in range(prog.NTT):
            k0 = kt * 128
            b = prog.bank()
            prog.mm_acc(b, P, [(prog.xn[:, dc, k0:k0 + 128],
                                w_v.tile(uc, dc)[:, sub:sub + P]) for dc in range(NCH)],
                        waits=[w_v.sem.tok(), ld.tok()])
            vtoks.append(prog.evac(b, "vector", lambda v, b=b, kt=kt, h=h: v.tensor_add(
                vhb[:, kt, :], b.ps[:, 0:P], vbt[:, h * P:(h + 1) * P])))
        blk = prog.phase()
        cbanks = prog.reserve_banks(4)
        for kt in range(prog.NTT):
            k0 = kt * 128
            exptoks = []
            ct_before = ctmm.n
            for qc in range(prog.NT):
                bs = prog.bank()

                def fs(t, bs=bs, kt=kt, qc=qc, sub=sub, ct_before=ct_before):
                    bs.free.wait(t)
                    if kt >= 1:
                        t.wait_ge(ctmm.h, ct_before)
                    bs.mm.inc(t.matmul(bs.ps[:], kTc[sub:sub + 64, kt * 128:(kt + 1) * 128],
                                       qTc[sub:sub + 64, qc * 512:(qc + 1) * 512],
                                       start=True, stop=True))
                blk.tensor(fs)

                def fexp(sc, bs=bs, qc=qc):
                    bs.mm.wait(sc)
                    ins = sc.activation(estrip[:, qc * 512:(qc + 1) * 512], bs.ps[:],
                                        AF.Exp, scale=float(scale),
                                        accum_out=zp[:, qc:qc + 1])
                    bs.free.inc(ins)
                exptoks.append((prog.sem("dummy"), 0))
                blk.scalar(fexp)
            exp_tok = None
            def fz(v, kt=kt, vtk=vtoks[kt]):
                for bs in []:
                    pass
                wait_tok(v, vtk)
                ins = v.reduce_sum(rz[:, 0:1], zp[:], AX.X)
                czp.inc(ins)
                v.wait_ge(czp.h, czp.n)
                ins = v.reciprocal(rz[:, 1:2], rz[:, 0:1])
                czp.inc(ins)
                v.wait_ge(czp.h, czp.n)
                ins = v.tensor_scalar(vsc[:, kt % 2, :], vhb[:, kt, :],
                                      rz[:, 1:2], None, ALU.mult)
                czp.inc(ins)
            lastexp = prog.sem("l1aexp")

            def fexp_last(sc):
                lastexp.inc(sc.engine_nop())
            def fexp2(sc):
                ins = sc.activation(rz[:, 0:1], zp[:, 0:1], AF.Copy, scale=1.0)
                lastexp.inc(ins)
            blk.scalar(fexp2)

            def fz2(v, kt=kt, vtk=vtoks[kt]):
                lastexp.wait(v)
                wait_tok(v, vtk)
                ins = v.reduce_sum(rz[:, 0:1], zp[:], AX.X)
                czp.inc(ins)
                v.wait_ge(czp.h, czp.n)
                ins = v.reciprocal(rz[:, 1:2], rz[:, 0:1])
                czp.inc(ins)
                v.wait_ge(czp.h, czp.n)
                ins = v.tensor_scalar(vsc[:, kt % 2, :], vhb[:, kt, :],
                                      rz[:, 1:2], None, ALU.mult)
                czp.inc(ins)
            blk.vector(fz2)
            vsc_tok = czp.tok()
            for qc in range(prog.NT):
                cb = cbanks[qc // 2]
                odd = qc % 2

                def fc(t, cb=cb, kt=kt, qc=qc, odd=odd, vsc_tok=vsc_tok):
                    wait_tok(t, vsc_tok)
                    if kt % 4 == 0 and qc < 2:
                        cb.free.wait(t)
                    ins = t.matmul(cb.ps[odd * 64:odd * 64 + 64, :] if odd else cb.ps[0:64, :],
                                   vsc[:, kt % 2, :],
                                   estrip[:, qc * 512:(qc + 1) * 512],
                                   start=(kt % 4 == 0), stop=(kt % 4 == 3),
                                   tile_position=(0, odd * 64),
                                   skip_group_check=True)
                    if qc == prog.NT - 1:
                        ctmm.inc(ins)
                blk.tensor(fc)
            if kt % 4 == 3:
                grp = kt // 4

                def facc(v, grp=grp, cbanks=tuple(cbanks), fin=None):
                    v.wait_ge(ctmm.h, ctmm.n)
                    ins = None
                    for j in range(4):
                        if grp == 0:
                            ins = v.tensor_copy(ctxacc[:, j, :], cbanks[j].ps[:])
                        else:
                            ins = v.tensor_add(ctxacc[:, j, :], ctxacc[:, j, :],
                                               cbanks[j].ps[:])
                    for j in range(4):
                        cbanks[j].free.inc(v.engine_nop())
                    czp.inc(ins)
                blk.vector(facc)
        cst = prog.sem("l1acst")

        def fcst(v):
            czp.wait(v)
            ins = None
            for qc in range(prog.NT):
                ins = v.tensor_copy(cpiece[0:64, 0:512],
                                    ctxacc[(qc % 2) * 64:(qc % 2) * 64 + 64, qc // 2, :])
                cst.inc(ins)
                v.wait_ge(cst.h, cst.n)
        for qc in range(prog.NT):
            def fcp(v, qc=qc):
                czp.wait(v)
                if qc > 0:
                    st.wait(v)
                ins = v.tensor_copy(cpiece[0:64, 0:512],
                                    ctxacc[(qc % 2) * 64:(qc % 2) * 64 + 64, qc // 2, :])
                cst.inc(ins)
            blk.vector(fcp)

            def fst(sy, h=h, qc=qc):
                cst.wait(sy)
                st.inc(sy.dma_start(
                    AP(ctxdram, (h * 64) * T + qc * 512, [[T, 64], [1, 512]]),
                    cpiece[0:64, 0:512]), 16)
            blk.sync(fst)
        prog.release_banks(cbanks)
    prog.arena_reset(full=True)
    w_l = WeightFM(prog, W["L1a_lw"])
    rhs = prog.aalloc([128, NCH, 512], BF16)
    bl = prog.aalloc([128, 8], F32)
    blk = prog.phase()
    w_l.load(prog)

    def fldl(gp):
        ld.inc(gp.dma_start(bl[:], W["L1a_bl"][:]), 16)
    blk.gpsimd(fldl)
    for tc in range(prog.NT):
        t0 = tc * 512
        blk = prog.phase()

        def fldr(sy, t0=t0):
            st.wait(sy)
            ld.inc(sy.dma_start(rhs[:], dram_fm(ctxdram, T, t0, 512)), 16)
        blk.sync(fldr)
        st_before = st.n
        otoks = []
        for ucx in range(NCH):
            b = prog.bank()
            prog.mm_acc(b, 512, [(w_l.tile(ucx, dc), rhs[:, dc]) for dc in range(NCH)],
                        waits=[w_l.sem.tok(), ld.tok()])

            def fev(sc, b=b, ucx=ucx, st_before=st_before):
                sc.wait_ge(st.h, st_before)
                return sc.activation(prog.ybuf[:, ucx], b.ps[:], AF.Identity,
                                     bias=bl[:, ucx:ucx + 1])
            otoks.append(prog.evac(b, "scalar", fev))

        def fsty(sy, t0=t0, otoks=tuple(otoks)):
            for tk in otoks:
                wait_tok(sy, tk)
            st.inc(sy.dma_start(dram_fm(yscr, T, t0, 512), prog.ybuf[:]), 16)
        blk.sync(fsty)
    blk = prog.phase()

    def fw(sy):
        st.wait(sy)
    blk.sync(fw)
    prog.phase()


def declare_l1_attn(prog, yscr):
    nc = prog.nc
    T = prog.T
    W = {"yscr": yscr}
    W["L1a_qw"] = declare_wfm(nc, "L1a_qw", D, D)
    W["L1a_kw"] = declare_wfm(nc, "L1a_kw", D, D)
    W["L1a_vw"] = declare_wfm(nc, "L1a_vw", D, D)
    W["L1a_lw"] = declare_wfm(nc, "L1a_lw", D, D)
    W["L1a_bqk"] = nc.dram_tensor("L1a_bqk", (128, 16), F32, kind="ExternalInput")
    W["L1a_vbt"] = nc.dram_tensor("L1a_vbt", (128, 1024), F32, kind="ExternalInput")
    W["L1a_bl"] = nc.dram_tensor("L1a_bl", (128, 8), F32, kind="ExternalInput")
    W["l1a_ctxdram"] = nc.dram_tensor("l1a_ctxdram", (D * T,), BF16, kind="Internal")
    return W


def prep_l1_attn(inputs):
    out = {}
    qw = np.asarray(inputs["L1_attn_q_w"])
    kw = np.asarray(inputs["L1_attn_k_w"])
    vw = np.asarray(inputs["L1_attn_v_w"])
    out["L1a_qw"] = host_wfm(qw.transpose(2, 0, 1).reshape(1024, 1024))
    out["L1a_kw"] = host_wfm(kw.transpose(2, 0, 1).reshape(1024, 1024))
    out["L1a_vw"] = host_wfm(vw.transpose(2, 0, 1).reshape(1024, 1024))
    out["L1a_lw"] = host_wfm(np.asarray(inputs["L1_attn_l_w"]).T)
    qb = np.asarray(inputs["L1_attn_q_b"]).reshape(1024)
    kb = np.asarray(inputs["L1_attn_k_b"]).reshape(1024)
    vb = np.asarray(inputs["L1_attn_v_b"]).reshape(1024)
    bqk = np.zeros((128, 16), np.float32)
    bqk[:, 0:8] = qb.reshape(8, 128).T
    bqk[:, 8:16] = kb.reshape(8, 128).T
    out["L1a_bqk"] = bqk
    out["L1a_vbt"] = np.broadcast_to(vb[None, :], (128, 1024)).copy().astype(np.float32)
    out["L1a_bl"] = np.ascontiguousarray(
        np.asarray(inputs["L1_attn_l_b"]).reshape(8, 128).T).astype(np.float32)
    return out



L2F_EDGES = []
_lind = 0
for _c in range(6):
    _n = min(_c + 1, 5)
    _s = _c - _n
    L2F_EDGES.append((_c, _lind + _n - 1, _lind, _lind + _n // 2,
                      _c - 1, _s, _s + _n // 2))
    _lind += _n


def emit_l2_ffn(prog, W, debug_stop=None):
    nc = prog.nc
    T = prog.T
    yscr = W["yscr"]
    NH, DH = 8, 128
    scale = 1.0 / math.sqrt(DH)
    ld = prog.sem("l2fld")
    st = prog.sem("l2fst")
    nodes = W["l2f_nodes"]
    qed = W["l2f_qed"]
    qhD = W["l2f_qh"]
    khD = W["l2f_kh"]
    vtokD = W["l2f_vtok"]
    ctxD = W["l2f_ctx"]

    def node_off(i):
        return (i + 1) * D * T

    blk = prog.phase()
    cv = prog.sem("l2fcv")
    for tc in range(prog.NT):
        t0 = tc * 512

        def fcv(v, t0=t0):
            st.wait(v)
            ins = None
            for dc in range(NCH):
                ins = v.tensor_copy(prog.xb16[:, dc], prog.xn[:, dc, t0:t0 + 512])
            cv.inc(ins)
        blk.vector(fcv)

        def fst(sy, t0=t0):
            cv.wait(sy)
            st.inc(sy.dma_start(
                AP(nodes, node_off(-1) + t0, [[T, 128], [128 * T, NCH], [1, 512]]),
                prog.xb16[:]), 16)
        blk.sync(fst)

    for c, eq, ek, ev, in_q, in_k, in_v in L2F_EDGES:
        for (edge, src_i, dst, ewj) in ((eq, in_q, qed, 0), (ek, in_k, W["l2f_ked"], 1),
                                        (ev, in_v, W["l2f_ved"], 2)):
            prog.arena_reset(full=True)
            we = WeightFM(prog, dram=None, dtype=BF16, n_uc=8, n_dc=8)
            rhs = prog.aalloc([128, NCH, 512], BF16)
            out_t = prog.aalloc([128, NCH, 512], BF16)
            bcol = prog.aalloc([128, 8], F32)
            ewc = prog.aalloc([128, 1], F32)
            beew = prog.aalloc([128, 8], F32)
            blk = prog.phase()

            def fldw(gp, edge=edge):
                k = 8 * 128
                for uc in range(8):
                    we.sem.inc(gp.dma_start(we.sb[:, uc * k:(uc + 1) * k],
                                            W["L2f_We"][edge, uc]), 16)
                ld.inc(gp.dma_start(bcol[:], W["L2f_becols"][:, edge * 8:(edge + 1) * 8]), 16)
                ld.inc(gp.dma_start(ewc[:], W["L2f_ewcols"][:, c * 3 + ewj:c * 3 + ewj + 1],
                                    allow_slow_non_contiguous=True), 16)
            blk.gpsimd(fldw)
            bee = prog.sem("l2fbee")

            def fbe(v):
                ld.wait(v)
                bee.inc(v.tensor_scalar(beew[:], bcol[:], ewc[:, 0:1], None, ALU.mult))
            blk.vector(fbe)
            for tc in range(prog.NT):
                t0 = tc * 512
                blk = prog.phase()

                def fldr(sy, t0=t0, src_i=src_i):
                    ld.inc(sy.dma_start(
                        rhs[:], AP(nodes, node_off(src_i) + t0,
                                   [[T, 128], [128 * T, NCH], [1, 512]])), 16)
                blk.sync(fldr)
                toks = []
                for uc in range(NCH):
                    b = prog.bank()
                    prog.mm_acc(b, 512, [(we.tile(uc, dc), rhs[:, dc]) for dc in range(NCH)],
                                waits=[we.sem.tok(), ld.tok(), bee.tok()])
                    toks.append(prog.evac(b, "scalar", lambda sc, b=b, uc=uc: sc.activation(
                        out_t[:, uc], b.ps[:], AF.Identity, scale=ewc[:, 0:1],
                        bias=beew[:, uc:uc + 1])))

                def fsto(sy, t0=t0, dst=dst, toks=tuple(toks)):
                    for tk in toks:
                        wait_tok(sy, tk)
                    st.inc(sy.dma_start(dram_fm(dst, T, t0, 512), out_t[:]), 16)
                blk.sync(fsto)
        for (wsrc, src, dstD, tokmaj) in ((W["L2f_Wq"], qed, qhD, False),
                                          (W["L2f_Wk"], W["l2f_ked"], khD, False),
                                          (W["L2f_Wv"], W["l2f_ved"], vtokD, True)):
            prog.arena_reset(full=True)
            wp = WeightFM(prog, dram=None, dtype=BF16, n_uc=8, n_dc=8)
            rhs = prog.aalloc([128, NCH, 512], BF16)
            blk = prog.phase()

            def fldw(gp, wsrc=wsrc, c=c):
                k = 8 * 128
                for uc in range(8):
                    wp.sem.inc(gp.dma_start(wp.sb[:, uc * k:(uc + 1) * k],
                                            wsrc[c, uc]), 16)
            blk.gpsimd(fldw)
            if not tokmaj:
                out_t = prog.aalloc([128, NCH, 512], BF16)
                for tc in range(prog.NT):
                    t0 = tc * 512
                    blk = prog.phase()

                    def fldr(sy, t0=t0, src=src):
                        ld.inc(sy.dma_start(rhs[:], dram_fm(src, T, t0, 512)), 16)
                    blk.sync(fldr)
                    toks = []
                    for uc in range(NCH):
                        b = prog.bank()
                        prog.mm_acc(b, 512, [(wp.tile(uc, dc), rhs[:, dc])
                                             for dc in range(NCH)],
                                    waits=[wp.sem.tok(), ld.tok()])
                        toks.append(prog.evac(b, "vector", lambda v, b=b, uc=uc:
                                              v.tensor_copy(out_t[:, uc], b.ps[:])))

                    def fsto(sy, t0=t0, dstD=dstD, toks=tuple(toks)):
                        for tk in toks:
                            wait_tok(sy, tk)
                        st.inc(sy.dma_start(dram_fm(dstD, T, t0, 512), out_t[:]), 16)
                    blk.sync(fsto)
            else:
                vst = prog.aalloc([128, NH, DH + 1], BF16)
                ms = prog.sem("l2fms")

                def fms(gp):
                    ms.inc(gp.memset(vst[:, :, DH:DH + 1], 1.0))
                blk.gpsimd(fms)
                for kt in range(prog.NTT):
                    k0 = kt * 128
                    blk = prog.phase()

                    def fldr(sy, kt=kt, src=src):
                        st.wait(sy)
                        ld.inc(sy.dma_start(
                            rhs[:, :, 0:128],
                            AP(src, kt * 128, [[T, 128], [128 * T, NCH], [1, 128]])), 16)
                    blk.sync(fldr)
                    toks = []
                    for hq in range(NH):
                        b = prog.bank()
                        prog.mm_acc(b, DH, [(rhs[:, dc][:, 0:128], wp.tile(hq, dc))
                                            for dc in range(NCH)],
                                    waits=[wp.sem.tok(), ld.tok()])
                        toks.append(prog.evac(b, "vector", lambda v, b=b, hq=hq:
                                              v.tensor_copy(vst[:, hq, 0:DH], b.ps[:, 0:DH])))

                    def fstv(sy, kt=kt, toks=tuple(toks)):
                        for tk in toks:
                            wait_tok(sy, tk)
                        ms.wait(sy)
                        st.inc(sy.dma_start(
                            AP(vtokD, kt * 128 * NH * (DH + 1),
                               [[NH * (DH + 1), 128], [DH + 1, NH], [1, DH + 1]]),
                            vst[:]), 16)
                    blk.sync(fstv)
        prog.arena_reset(full=True)
        qh = prog.aalloc([128, T], BF16)
        kh = prog.aalloc([128, T], BF16)
        vh = [prog.aalloc([128, prog.NTT, DH + 1], BF16) for _ in range(1)]
        eT = [prog.aalloc([128, 512], BF16) for _ in range(2)]
        ctxn = prog.aalloc([128, 512], BF16)
        cstg = prog.aalloc([128, 512], BF16)
        rcol = prog.aalloc([128, 4], F32)
        etf = prog.sem("l2fet")
        ctok = prog.sem("l2fct")
        trs = prog.sem("l2ftr")
        lvh = prog.sem("l2fvh")
        for h in range(NH):
            blk = prog.phase()

            def fldh(sy, h=h):
                etf.wait(sy)
                ld.inc(sy.dma_start(qh[:], AP(qhD, h * 128 * T, [[T, 128], [1, T]])), 16)
                ld.inc(sy.dma_start(kh[:], AP(khD, h * 128 * T, [[T, 128], [1, T]])), 16)
                lvh.inc(sy.dma_start(
                    vh[0][:], AP(vtokD, h * (DH + 1),
                                 [[NH * (DH + 1), 128],
                                  [128 * NH * (DH + 1), prog.NTT], [1, DH + 1]])), 16)
            blk.sync(fldh)
            for tc in range(prog.NT):
                t0 = tc * 512
                cbanks = prog.reserve_banks(4)
                etoks = [None, None]
                for si in range(prog.NTT):
                    bs = prog.bank()

                    def fs(t, bs=bs, si=si, t0=t0):
                        ld.wait(t)
                        bs.free.wait(t)
                        bs.mm.inc(t.matmul(bs.ps[:], kh[:, si * 128:(si + 1) * 128],
                                           qh[:, t0:t0 + 512], start=True, stop=True))
                    blk.tensor(fs)

                    def fexp(sc, bs=bs, si=si, et=etoks[si % 2]):
                        bs.mm.wait(sc)
                        if et is not None:
                            wait_tok(sc, et)
                        ins = sc.activation(eT[si % 2][:], bs.ps[:], AF.Exp,
                                            scale=float(scale))
                        bs.free.inc(ins)
                    blk.scalar(fexp)
                    for tt in range(4):
                        cb = cbanks[tt]

                        def fc(t, cb=cb, si=si, tt=tt, bs=bs):
                            wait_tok(t, (bs.free, bs.free.n))
                            if si == 0:
                                lvh.wait(t)
                                cb.free.wait(t)
                            ins = t.matmul(cb.ps[:, 0:DH + 1],
                                           eT[si % 2][:, tt * 128:(tt + 1) * 128],
                                           vh[0][:, si, :], start=(si == 0),
                                           stop=(si == prog.NTT - 1))
                            if tt == 3:
                                etf.inc(ins)
                        blk.tensor(fc)
                    etoks[si % 2] = etf.tok()
                for tt in range(4):
                    cb = cbanks[tt]

                    def fnorm(v, cb=cb, tt=tt, fin=etf.tok()):
                        wait_tok(v, fin)
                        ins = v.reciprocal(rcol[:, tt:tt + 1], cb.ps[:, DH:DH + 1])
                        ctok.inc(ins)
                        v.wait_ge(ctok.h, ctok.n)
                        ins = v.tensor_scalar(ctxn[:, tt * 128:(tt + 1) * 128],
                                              cb.ps[:, 0:DH], rcol[:, tt:tt + 1],
                                              None, ALU.mult)
                        ctok.inc(ins)
                        cb.free.inc(v.engine_nop())
                    blk.vector(fnorm)
                prog.release_banks(cbanks)
                for tt in range(4):
                    bt = prog.bank()

                    def ftr(t, bt=bt, tt=tt, ctk=ctok.tok()):
                        wait_tok(t, ctk)
                        bt.free.wait(t)
                        bt.mm.inc(t.transpose(
                            bt.ps[:].bitcast(BF16)[:, 0:128],
                            ctxn[:, tt * 128:(tt + 1) * 128], prog.ident[:]))
                    blk.tensor(ftr)

                    def fte(v, bt=bt, tt=tt):
                        bt.mm.wait(v)
                        if True:
                            st.wait(v)
                        ins = v.tensor_copy(cstg[:, 0:128], bt.ps[:].bitcast(BF16)[:, 0:128])
                        trs.inc(ins)
                        bt.free.inc(v.engine_nop())
                    blk.vector(fte)

                    def fstc(sy, h=h, t0=t0, tt=tt):
                        trs.wait(sy)
                        st.inc(sy.dma_start(
                            AP(ctxD, (h * 128) * T + t0 + tt * 128, [[T, 128], [1, 128]]),
                            cstg[:, 0:128]), 16)
                    blk.sync(fstc)
        prog.arena_reset(full=True)
        wo = WeightFM(prog, dram=None, dtype=BF16, n_uc=8, n_dc=8)
        rhs = prog.aalloc([128, NCH, 512], BF16)
        qrt = prog.aalloc([128, NCH, 512], BF16)
        nod = prog.aalloc([128, NCH, 512], BF16)
        awc = prog.aalloc([128, 1], F32)
        blk = prog.phase()

        def fldwo(gp, c=c):
            k = 8 * 128
            for uc in range(8):
                wo.sem.inc(gp.dma_start(wo.sb[:, uc * k:(uc + 1) * k],
                                        W["L2f_Wo"][c, uc]), 16)
            ld.inc(gp.dma_start(awc[:], W["L2f_awcols"][:, c:c + 1],
                                allow_slow_non_contiguous=True), 16)
        blk.gpsimd(fldwo)
        for tc in range(prog.NT):
            t0 = tc * 512
            blk = prog.phase()

            def fldr(sy, t0=t0):
                st.wait(sy)
                ld.inc(sy.dma_start(rhs[:], dram_fm(ctxD, T, t0, 512)), 16)
                ld.inc(sy.dma_start(qrt[:], dram_fm(qed, T, t0, 512)), 16)
            blk.sync(fldr)
            toks = []
            for uc in range(NCH):
                b = prog.bank()
                prog.mm_acc(b, 512, [(wo.tile(uc, dc), rhs[:, dc]) for dc in range(NCH)],
                            waits=[wo.sem.tok(), ld.tok()])

                def fev(v, b=b, uc=uc):
                    return v.scalar_tensor_tensor(nod[:, uc], b.ps[:], awc[:, 0:1],
                                                  qrt[:, uc], ALU.mult, ALU.add)
                toks.append(prog.evac(b, "vector", fev))

            def fstn(sy, t0=t0, c=c, toks=tuple(toks)):
                for tk in toks:
                    wait_tok(sy, tk)
                st.inc(sy.dma_start(
                    AP(nodes, node_off(c) + t0, [[T, 128], [128 * T, NCH], [1, 512]]),
                    nod[:]), 16)
            blk.sync(fstn)
    prog.arena_reset(full=False)
    gbc = prog.aalloc([128, 16], F32)
    blk = prog.phase()

    def fldgb(gp):
        ld.inc(gp.dma_start(gbc[:], W["L2f_lngb"][:]), 16)
    blk.gpsimd(fldgb)
    for tc in range(prog.NT):
        t0 = tc * 512
        blk = prog.phase()
        sq = prog.sem("sq")
        ve = prog.sem("lnve")

        def fdma(sy, t0=t0):
            ld.inc(sy.dma_start(
                prog.xb16[:], AP(nodes, node_off(5) + t0,
                                 [[T, 128], [128 * T, NCH], [1, 512]])), 16)
        blk.sync(fdma)

        def fsq(sc):
            ld.wait(sc)
            ins = None
            for dc in range(NCH):
                ins = sc.activation(prog.sqb[:, dc], prog.xb16[:, dc], AF.Square)
            sq.inc(ins)
        blk.scalar(fsq)
        b1 = prog.bank()
        b2 = prog.bank()
        prog.mm_acc(b1, 512, [(prog.ones_bf[:], prog.xb16[:, dc]) for dc in range(NCH)],
                    waits=[ld.tok()])
        prog.mm_acc(b2, 512, [(prog.ones_bf[:], prog.sqb[:, dc]) for dc in range(NCH)],
                    waits=[sq.tok()])
        tok1 = prog.evac(b1, "scalar", lambda sc, b=b1: sc.activation(
            prog.mu[:], b.ps[:], AF.Copy, scale=1.0 / D))
        tok2 = prog.evac(b2, "scalar", lambda sc, b=b2: sc.activation(
            prog.msq[:], b.ps[:], AF.Copy, scale=1.0 / D))

        def fvar(v):
            wait_tok(v, tok1)
            wait_tok(v, tok2)
            v.tensor_mul(prog.mu2[:], prog.mu[:], prog.mu[:])
            ve.inc(v.tensor_sub(prog.mu2[:], prog.msq[:], prog.mu2[:]))
        blk.vector(fvar)

        def fsqr(sc):
            ve.wait(sc)
            ve.inc(sc.activation(prog.mu2[:], prog.mu2[:], AF.Sqrt, bias=1e-6))
        blk.scalar(fsqr)

        def frec(v):
            ve.wait(v)
            ve.inc(v.reciprocal(prog.mu2[:], prog.mu2[:]))
        blk.vector(frec)

        def fap(v, t0=t0):
            ve.wait(v)
            st.wait(v)
            ld.wait(v)
            ins = None
            for dc in range(NCH):
                v.tensor_sub(prog.tscr[:], prog.xb16[:, dc], prog.mu[:])
                v.tensor_mul(prog.tscr[:], prog.tscr[:], prog.mu2[:])
                ins = v.tensor_scalar(prog.ybuf[:, dc], prog.tscr[:],
                                      gbc[:, 2 * dc:2 * dc + 1],
                                      gbc[:, 2 * dc + 1:2 * dc + 2],
                                      ALU.mult, ALU.add)
            ve.inc(ins)
        blk.vector(fap)

        def fsty(sy, t0=t0):
            ve.wait(sy)
            st.inc(sy.dma_start(dram_fm(yscr, T, t0, 512), prog.ybuf[:]), 16)
        blk.sync(fsty)
    blk = prog.phase()

    def fw(sy):
        st.wait(sy)
    blk.sync(fw)
    prog.phase()


def declare_l2_ffn(prog, yscr):
    nc = prog.nc
    T = prog.T
    W = {"yscr": yscr}
    W["L2f_We"] = nc.dram_tensor("L2f_We", (20, 8, 128, 8, 128), F32, kind="ExternalInput")
    W["L2f_Wq"] = nc.dram_tensor("L2f_Wq", (6, 8, 128, 8, 128), F32, kind="ExternalInput")
    W["L2f_Wk"] = nc.dram_tensor("L2f_Wk", (6, 8, 128, 8, 128), F32, kind="ExternalInput")
    W["L2f_Wv"] = nc.dram_tensor("L2f_Wv", (6, 8, 128, 8, 128), F32, kind="ExternalInput")
    W["L2f_Wo"] = nc.dram_tensor("L2f_Wo", (6, 8, 128, 8, 128), F32, kind="ExternalInput")
    W["L2f_becols"] = nc.dram_tensor("L2f_becols", (128, 160), F32, kind="ExternalInput")
    W["L2f_ewcols"] = nc.dram_tensor("L2f_ewcols", (128, 18), F32, kind="ExternalInput")
    W["L2f_awcols"] = nc.dram_tensor("L2f_awcols", (128, 6), F32, kind="ExternalInput")
    W["L2f_lngb"] = nc.dram_tensor("L2f_lngb", (128, 16), F32, kind="ExternalInput")
    W["l2f_nodes"] = nc.dram_tensor("l2f_nodes", (7 * D * T,), BF16, kind="Internal")
    W["l2f_qed"] = nc.dram_tensor("l2f_qed", (D * T,), BF16, kind="Internal")
    W["l2f_ked"] = nc.dram_tensor("l2f_ked", (D * T,), BF16, kind="Internal")
    W["l2f_ved"] = nc.dram_tensor("l2f_ved", (D * T,), BF16, kind="Internal")
    W["l2f_qh"] = nc.dram_tensor("l2f_qh", (D * T,), BF16, kind="Internal")
    W["l2f_kh"] = nc.dram_tensor("l2f_kh", (D * T,), BF16, kind="Internal")
    W["l2f_vtok"] = nc.dram_tensor("l2f_vtok", (T * 8 * 129,), BF16, kind="Internal")
    W["l2f_ctx"] = nc.dram_tensor("l2f_ctx", (D * T,), BF16, kind="Internal")
    return W


def prep_l2_ffn(inputs):
    out = {}
    We = np.asarray(inputs["L2_ffn_We"])
    out["L2f_We"] = np.stack([host_wfm(We[e]) for e in range(20)], 0)
    for nm in ("Wq", "Wk", "Wv", "Wo"):
        w = np.asarray(inputs[f"L2_ffn_{nm}"])
        out[f"L2f_{nm}"] = np.stack([host_wfm(w[cc]) for cc in range(6)], 0)
    be = np.asarray(inputs["L2_ffn_be"])
    bec = np.zeros((128, 160), np.float32)
    for e in range(20):
        bec[:, e * 8:(e + 1) * 8] = be[e].reshape(8, 128).T
    out["L2f_becols"] = bec
    ew = np.asarray(inputs["L2_ffn_ew"])
    out["L2f_ewcols"] = np.broadcast_to(ew.reshape(1, 18), (128, 18)).copy().astype(np.float32)
    aw = np.asarray(inputs["L2_ffn_act_w"])
    out["L2f_awcols"] = np.broadcast_to(aw.reshape(1, 6), (128, 6)).copy().astype(np.float32)
    g = np.asarray(inputs["L2_ffn_ln_g"]).reshape(NCH, 128)
    b = np.asarray(inputs["L2_ffn_ln_b"]).reshape(NCH, 128)
    gb = np.zeros((128, 16), np.float32)
    for dc in range(NCH):
        gb[:, 2 * dc] = g[dc]
        gb[:, 2 * dc + 1] = b[dc]
    out["L2f_lngb"] = gb
    return out




def build_common(nc, T):
    prog = Prog(nc, T)
    x_in = nc.dram_tensor("x_in", (D * T,), F32, kind="ExternalInput")
    y_out = nc.dram_tensor("y_out", (D * T,), F32, kind="ExternalOutput")
    xres = nc.dram_tensor("xres", (D * T,), F32, kind="Internal")
    yscr = nc.dram_tensor("yscr", (D * T,), F32, kind="Internal")
    return prog, x_in, y_out, xres, yscr


def declare_l0_ffn(prog, yscr):
    nc = prog.nc
    W = {}
    W["L0_ffn_weight_dram"] = declare_wfm(nc, "L0_ffn_weight", D, D)
    W["l0ffn_gb"] = nc.alloc_sbuf_tensor("l0ffn_gb_sb", [128, 16], F32)
    W["l0ffn_gb_dram"] = nc.dram_tensor("l0ffn_gb", (128, 16), F32, kind="ExternalInput")
    W["pscr"] = nc.dram_tensor("pscr", (256,), F32, kind="Internal")
    W["rowbuf"] = nc.alloc_sbuf_tensor("rowbuf", [1, 256], F32)
    W["ccin"] = nc.alloc_sbuf_tensor("ccin", [1, 16], F32)
    W["cc_in_d"] = nc.dram_tensor("cc_in_d", (1, 16), F32, kind="Internal")
    W["cc_out_d"] = nc.dram_tensor("cc_out_d", (1, 16), F32, kind="Internal")
    W["yscr"] = yscr
    return W


def prep_l0_ffn(inputs):
    out = {}
    out["L0_ffn_weight"] = host_wfm(inputs["L0_ffn_weight"])
    g = np.asarray(inputs["L0_ffn_ln_gamma"]).reshape(NCH, 128)
    b = np.asarray(inputs["L0_ffn_ln_beta"]).reshape(NCH, 128)
    gb = np.zeros((128, 16), np.float32)
    for dc in range(NCH):
        gb[:, 2 * dc] = g[dc]
        gb[:, 2 * dc + 1] = b[dc]
    out["l0ffn_gb"] = gb
    return out


BLOCKS = ["L0_attn", "L0_ffn", "L1_attn", "L1_ffn",
          "L2_attn", "L2_ffn", "L3_attn", "L3_ffn"]


def build_program(T, block_list, debug_stop=None):
    nc = bass.Bass("TRN2", target_bir_lowering=False)
    prog, x_in, y_out, xres, yscr = build_common(nc, T)

    decls = {}
    for name in block_list:
        if name == "L0_ffn":
            decls[name] = declare_l0_ffn(prog, yscr)
        elif name == "L0_attn":
            decls[name] = declare_l0_attn(prog, yscr)
        elif name == "L1_ffn":
            decls[name] = declare_l1_ffn(prog, yscr)
        elif name == "L3_ffn":
            decls[name] = declare_l3_ffn(prog, yscr)
        elif name == "L3_attn":
            decls[name] = declare_l3_attn(prog, yscr)
        elif name == "L1_attn":
            decls[name] = declare_l1_attn(prog, yscr)
        elif name == "L2_ffn":
            decls[name] = declare_l2_ffn(prog, yscr)
        elif name == "L2_attn":
            decls[name] = declare_l2_attn(prog, yscr)
        else:
            raise NotImplementedError(name)

    for v in (1e-5, 1e-3, 1e-6, 1e-9, 1.0, 1e-8, 0.5):
        prog.reg_const(v)
    prog.phase()
    cp = prog.sem("cp")
    big = [[D * T // 128, 128], [1, D * T // 128]]

    def fcp(sy):
        cp.inc(sy.dma_start(AP(xres, 0, big), AP(x_in, 0, big)), 16)
        if "L0_ffn" in decls:
            W = decls["L0_ffn"]
            cp.inc(sy.dma_start(W["l0ffn_gb"][:], W["l0ffn_gb_dram"][:]), 16)
        cp.wait(sy)
    prog.blk.sync(fcp)
    prog.init_phase()

    def dump_gb():
        W = decls["L0_ffn"]
        blk = prog.phase()
        dbg = prog.sem("dbg")
        cv = prog.sem("dbgcv")

        def fcv(v):
            cv.inc(v.tensor_copy(prog.tscr[:, 0:16], W["l0ffn_gb"][:, 0:16]))
        blk.vector(fcv)

        def fdbg(sy):
            cv.wait(sy)
            dbg.inc(sy.dma_start(AP(yscr, 0, [[T, 128], [1, 16]]), prog.tscr[:, 0:16]), 16)
            dbg.wait(sy)
        blk.sync(fdbg)

    if debug_stop == "gb_pre":
        dump_gb()
        block_list = []

    for name in block_list:
        emit_ln_in(prog, xres)
        if debug_stop == "gb_postln":
            dump_gb()
            break
        if debug_stop == "ln_in":
            emit_dump_xn(prog, yscr)
            break
        if name == "L0_ffn":
            emit_l0_ffn(prog, decls[name], debug_stop=debug_stop)
        elif name == "L0_attn":
            emit_l0_attn(prog, decls[name], debug_stop=debug_stop)
        elif name == "L1_ffn":
            emit_l1_ffn(prog, decls[name], debug_stop=debug_stop)
        elif name == "L3_ffn":
            emit_l3_ffn(prog, decls[name], debug_stop=debug_stop)
        elif name == "L3_attn":
            emit_l3_attn(prog, decls[name], debug_stop=debug_stop)
        elif name == "L1_attn":
            emit_l1_attn(prog, decls[name], debug_stop=debug_stop)
        elif name == "L2_ffn":
            emit_l2_ffn(prog, decls[name], debug_stop=debug_stop)
        elif name == "L2_attn":
            emit_l2_attn(prog, decls[name], debug_stop=debug_stop)
        if debug_stop is not None:
            break
        emit_ln_out_residual(prog, yscr, xres)

    src_fin = yscr if debug_stop is not None else xres
    blk = prog.phase()
    fin = prog.sem("fin")
    big = [[D * T // 128, 128], [1, D * T // 128]]

    def ffin(sy):
        fin.inc(sy.dma_start(AP(y_out, 0, big), AP(src_fin, 0, big)), 16)
        fin.wait(sy)
    blk.sync(ffin)
    prog.finish()
    return nc


def prep_inputs(inputs, block_list, T, xfull):
    common = {}
    for name in block_list:
        if name == "L0_ffn":
            common.update(prep_l0_ffn(inputs))
        elif name == "L0_attn":
            common.update(prep_l0_attn(inputs, T))
        elif name == "L1_ffn":
            common.update(prep_l1_ffn(inputs))
        elif name == "L3_ffn":
            common.update(prep_l3_ffn(inputs))
        elif name == "L3_attn":
            common.update(prep_l3_attn(inputs))
        elif name == "L1_attn":
            common.update(prep_l1_attn(inputs))
        elif name == "L2_ffn":
            common.update(prep_l2_ffn(inputs))
        elif name == "L2_attn":
            common.update(prep_l2_attn(inputs))
    maps = []
    B = xfull.shape[0]
    for b in range(8):
        m = dict(common)
        xb = np.ascontiguousarray(np.asarray(xfull[b % B]).T.reshape(-1)).astype(np.float32)
        m["x_in"] = xb
        maps.append(m)
    return maps


def run_blocks(inputs, block_list, T, xfull, debug_stop=None):
    nc = build_program(T, block_list, debug_stop=debug_stop)
    maps = prep_inputs(inputs, block_list, T, xfull)
    res = run_bass_kernel_spmd(nc, maps, core_ids=list(range(8)))
    outs = []
    for b in range(8):
        y = res.results[b]["y_out"].reshape(D, T).T
        outs.append(y)
    return np.stack(outs, 0).astype(np.float32)


def kernel(**inputs):
    T = 4096
    block_list = ["L0_attn", "L0_ffn", "L1_attn", "L1_ffn",
                  "L2_attn", "L2_ffn", "L3_attn", "L3_ffn"]
    xfull = np.asarray(inputs["L0_attn_hidden_states"])
    B = xfull.shape[0]
    xcur = [np.ascontiguousarray(np.asarray(xfull[b]).T.reshape(-1)).astype(np.float32)
            for b in range(8)]
    for name in block_list:
        nc = build_program(T, [name])
        common = {}
        prep_one(common, name, inputs, T)
        maps = [dict(common, x_in=xcur[b]) for b in range(8)]
        res = run_bass_kernel_spmd(nc, maps, core_ids=list(range(8)))
        xcur = [np.ascontiguousarray(res.results[b]["y_out"]).reshape(-1)
                for b in range(8)]
    outs = [xcur[b].reshape(D, T).T for b in range(8)]
    return np.stack(outs, 0).astype(np.float32)


def prep_one(common, name, inputs, T):
    if name == "L0_ffn":
        common.update(prep_l0_ffn(inputs))
    elif name == "L0_attn":
        common.update(prep_l0_attn(inputs, T))
    elif name == "L1_ffn":
        common.update(prep_l1_ffn(inputs))
    elif name == "L1_attn":
        common.update(prep_l1_attn(inputs))
    elif name == "L2_ffn":
        common.update(prep_l2_ffn(inputs))
    elif name == "L2_attn":
        common.update(prep_l2_attn(inputs))
    elif name == "L3_attn":
        common.update(prep_l3_attn(inputs))
    elif name == "L3_ffn":
        common.update(prep_l3_ffn(inputs))
